# version 29
# baseline (speedup 1.0000x reference)
import numpy as np
import ml_dtypes
from contextlib import ExitStack

import concourse.bass as bass
import concourse.mybir as mybir
from concourse.bass_utils import run_bass_kernel_spmd

F32 = mybir.dt.float32
BF16 = mybir.dt.bfloat16
F8 = mybir.dt.float8e4
AF = mybir.ActivationFunctionType
ALU = mybir.AluOpType
AX = mybir.AxisListType

D = 1024
NEG = -1.0e30


class _Null:
    def __getattr__(self, k):
        return lambda *a, **kw: None


_NULL = _Null()


class Env:
    def __init__(self, sems):
        self.sems = sems
        self.counts = {k: 0 for k in sems}

    def inc(self, name, instr, n=1):
        if instr is not None:
            instr.then_inc(self.sems[name], n)
        self.counts[name] += n
        return self.counts[name]

    def val(self, name):
        return self.counts[name]


SEM_NAMES = ["dX", "dX2", "dW", "dW2", "dS", "dS2", "dS3", "dG", "dG2", "eT", "eV", "eS", "eG"]


def run_phase(nc, sems, body):
    with nc.Block() as blk:
        blk.sync(lambda e: body(Env(sems), "sync", e))
        blk.gpsimd(lambda e: body(Env(sems), "gpsimd", e))
        blk.vector(lambda e: body(Env(sems), "vector", e))
        blk.scalar(lambda e: body(Env(sems), "scalar", e))
        blk.tensor(lambda e: body(Env(sems), "tensor", e))


def proxies(cur, eng):
    Y = lambda n: eng if cur == n else _NULL
    return Y("sync"), Y("gpsimd"), Y("vector"), Y("scalar"), Y("tensor")


def mk_waiter(env, cur, eng):
    def w(name, sem, v=None):
        if cur == name:
            eng.wait_ge(env.sems[sem], env.val(sem) if v is None else v)
    return w




def fp4_chain(env, V, w, src_ap, pw, B, pre_wait=None):
    nblk = pw // 16
    if pre_wait is not None:
        pre_wait()
    V.tensor_reduce(B["qamax"][:, 0:nblk],
                    src_ap.rearrange("p (b k) -> p b k", k=16),
                    AX.X, ALU.max, apply_absolute_value=True)
    V.drain()
    V.tensor_scalar(B["qscale"][:, 0:nblk], B["qamax"][:, 0:nblk],
                    0.16666666666666666, 1e-12, ALU.mult, ALU.max)
    V.drain()
    V.reciprocal(B["qrecip"][:, 0:nblk], B["qscale"][:, 0:nblk])
    V.drain()
    V.tensor_copy(B["qrecipb"][:, 0:nblk], B["qrecip"][:, 0:nblk])
    V.drain()
    V.tensor_scalar(B["qscaleb"][:, 0:nblk], B["qscale"][:, 0:nblk], 0.5, None, ALU.mult)
    V.drain()
    V.tensor_tensor(B["qxn"][:, 0:pw].rearrange("p (b k) -> p b k", k=16),
                    src_ap.rearrange("p (b k) -> p b k", k=16),
                    B["qrecipb"][:, 0:nblk].unsqueeze(-1).broadcast_to((128, nblk, 16)),
                    ALU.mult)
    V.drain()
    V.tensor_scalar(B["qtmp"][:, 0:pw], B["qxn"][:, 0:pw], 0.0, -2.0, ALU.is_lt, ALU.mult)
    V.drain()
    V.tensor_scalar(B["qtmp"][:, 0:pw], B["qtmp"][:, 0:pw], 1.0, None, ALU.add)
    V.drain()
    V.tensor_tensor(B["qt"][:, 0:pw], B["qxn"][:, 0:pw], B["qtmp"][:, 0:pw], ALU.mult)
    V.drain()
    V.tensor_scalar(B["qacc"][:, 0:pw], B["qt"][:, 0:pw], 5.0, 2.0, ALU.is_gt, ALU.mult)
    V.drain()
    V.scalar_tensor_tensor(B["qacc"][:, 0:pw], B["qt"][:, 0:pw], 2.5, B["qacc"][:, 0:pw], ALU.is_gt, ALU.add)
    V.drain()
    V.scalar_tensor_tensor(B["qacc"][:, 0:pw], B["qt"][:, 0:pw], 3.5, B["qacc"][:, 0:pw], ALU.is_gt, ALU.add)
    V.drain()
    V.tensor_scalar(B["qacc"][:, 0:pw], B["qacc"][:, 0:pw], 2.0, None, ALU.mult)
    V.drain()
    for mj in (0.25, 0.75, 1.25, 1.75):
        V.scalar_tensor_tensor(B["qacc"][:, 0:pw], B["qt"][:, 0:pw], mj, B["qacc"][:, 0:pw], ALU.is_gt, ALU.add)
        V.drain()
    V.tensor_tensor(B["qtmp"][:, 0:pw].rearrange("p (b k) -> p b k", k=16),
                    B["qtmp"][:, 0:pw].rearrange("p (b k) -> p b k", k=16),
                    B["qscaleb"][:, 0:nblk].unsqueeze(-1).broadcast_to((128, nblk, 16)),
                    ALU.mult)
    V.drain()
    return env.inc("eV", V.tensor_tensor(B["qxn"][:, 0:pw], B["qacc"][:, 0:pw],
                                         B["qtmp"][:, 0:pw], ALU.mult), 1)


def emit_ln(env, V, SC, w, x_ap, out_ap, stats, mv, rstd, nmr, eps, out_dtype_note=None):
    w("vector", "eS")
    V.bn_stats(stats[:, 0:6], x_ap[:, 0:512])
    V.bn_stats(stats[:, 6:12], x_ap[:, 512:1024])
    V.drain()
    V.bn_aggr(mv[:], stats[:])
    V.drain()
    env.inc("eV", V.tensor_scalar(rstd[:], mv[:, 1:2], eps, None, ALU.add), 1)
    w("scalar", "eV")
    env.inc("eS", SC.sqrt(rstd[:], rstd[:]), 1)
    w("vector", "eS")
    V.reciprocal(rstd[:], rstd[:])
    V.drain()
    env.inc("eV", V.tensor_scalar(nmr[:], mv[:, 0:1], rstd[:, 0:1], -1.0,
                                  ALU.mult, ALU.mult), 1)
    w("scalar", "eV")
    return env.inc("eS", SC.activation(out_ap, x_ap, AF.Identity,
                                       bias=nmr[:, 0:1], scale=rstd[:, 0:1]), 1)


def build_program(S=4096, stop_after=None):
    NT = S // 128
    nc = bass.Bass("TRN2", target_bir_lowering=False)

    x0_d = nc.dram_tensor("x0", [S, D], F32, kind="ExternalInput")
    wl0_d = nc.dram_tensor("w_l0", [2 * D, D], F32, kind="ExternalInput")
    k8_d = nc.dram_tensor("k8", [D, D], BF16, kind="ExternalInput")
    sc8_d = nc.dram_tensor("scales8", [1, 2], F32, kind="ExternalInput")
    b8_d = nc.dram_tensor("bias8", [1, D], BF16, kind="ExternalInput")
    wq_d = nc.dram_tensor("wq", [D, D], F32, kind="ExternalInput")
    wk_d = nc.dram_tensor("wk", [D, D], F32, kind="ExternalInput")
    wv_d = nc.dram_tensor("wv", [D, D], F32, kind="ExternalInput")
    bk_d = nc.dram_tensor("bk", [1, D], F32, kind="ExternalInput")
    bv_d = nc.dram_tensor("bv", [1, D], F32, kind="ExternalInput")
    wo_d = nc.dram_tensor("wo", [D, D], F32, kind="ExternalInput")
    bo_d = nc.dram_tensor("bo", [1, D], F32, kind="ExternalInput")
    w1_d = nc.dram_tensor("w1", [D, 4096], F32, kind="ExternalInput")
    b1_d = nc.dram_tensor("b1", [1, 4096], F32, kind="ExternalInput")
    w2_d = nc.dram_tensor("w2", [4096, D], F32, kind="ExternalInput")
    b2_d = nc.dram_tensor("b2", [1, D], F32, kind="ExternalInput")
    wg_d = nc.dram_tensor("wg", [2816, D], F32, kind="ExternalInput")
    wu_d = nc.dram_tensor("wu", [2816, D], F32, kind="ExternalInput")
    wd_d = nc.dram_tensor("wd", [D, 2816], F32, kind="ExternalInput")
    id_d = nc.dram_tensor("ident", [128, 128], BF16, kind="ExternalInput")
    mask_d = nc.dram_tensor("mask", [128, 128], F32, kind="ExternalInput")
    bq_c_d = nc.dram_tensor("bq_col", [128, 8], F32, kind="ExternalInput")
    g1_c_d = nc.dram_tensor("g1_col", [128, 8], F32, kind="ExternalInput")
    b1l_c_d = nc.dram_tensor("b1l_col", [128, 8], F32, kind="ExternalInput")
    g2_c_d = nc.dram_tensor("g2_col", [128, 8], F32, kind="ExternalInput")
    b2l_c_d = nc.dram_tensor("b2l_col", [128, 8], F32, kind="ExternalInput")

    out_d = nc.dram_tensor("out", [S, D], F32, kind="ExternalOutput")

    scr_x1 = nc.dram_tensor("scr_x1", [S, D], F32)
    scr_xn1 = nc.dram_tensor("scr_xn1", [S, D], F32)
    scr_xq4T = nc.dram_tensor("scr_xq4T", [8, 128, S], BF16)
    scr_phiqT = nc.dram_tensor("scr_phiqT", [64, 16, S], BF16)
    scr_p1 = nc.dram_tensor("scr_p1", [S, D], F32)
    scr_wgq = nc.dram_tensor("scr_wgq", [8, 128, 2816], BF16)
    scr_wuq = nc.dram_tensor("scr_wuq", [8, 128, 2816], BF16)
    scr_wdq = nc.dram_tensor("scr_wdq", [22, 128, D], BF16)
    scr_y0T = nc.dram_tensor("scr_y0T", [8, 128, S], BF16)
    scr_s2 = nc.dram_tensor("scr_s2", [S, D], F32)

    es = ExitStack()
    with es:
        SBo = lambda n, sh, dt: es.enter_context(nc.sbuf_tensor(n, sh, dt))
        ident = SBo("identS", [128, 128], BF16)
        onesb = SBo("onesb", [1, 128], BF16)
        ones_s = SBo("ones_s", [128, 1], BF16)
        c001 = SBo("c001", [128, 1], F32)
        allsems = {}
        for ph in ("P0", "AB", "C1", "C2", "D1", "D2", "TL"):
            allsems[ph] = {n: es.enter_context(nc.semaphore(ph + n)) for n in SEM_NAMES}

        l0es = ExitStack()
        with l0es:
            SL = lambda n, sh, dt: l0es.enter_context(nc.sbuf_tensor(n, sh, dt))
            maskt = SL("maskS", [128, 128], F32)
            WT = SL("WT", [128, 8, 2 * D], BF16)
            K8Q = SL("K8Q", [128, 8, D], F8)
            b8t = SL("b8t", [128, D], BF16)
            rs_in = SL("rs_in", [128, 1], F32)
            ss8 = SL("ss8", [128, 1], F32)

            p0es = ExitStack()
            with p0es:
                SP = lambda n, sh, dt: p0es.enter_context(nc.sbuf_tensor(n, sh, dt))
                w_oc = SP("w_oc", [128, 16, D], BF16)
                k8_sb = SP("k8_sb", [128, 8, D], BF16)
                sc_sb = SP("sc_sb", [1, 2], F32)
                sc_tmp = SP("sc_tmp", [1, 4], F32)
                rs_k = SP("rs_k", [128, 1], F32)
                b8row = SP("b8row", [1, D], BF16)
                onesf = SP("onesf", [1, 128], F32)
                PA = p0es.enter_context(nc.psum_tensor("p0A", [128, 1024], BF16))
                PB = p0es.enter_context(nc.psum_tensor("p0B", [128, 1024], BF16))
                PC = p0es.enter_context(nc.psum_tensor("p0C", [128, 512], F32))
                PD = p0es.enter_context(nc.psum_tensor("p0D", [128, 512], F32))

                def P0(env, cur, eng):
                    SY, GP, V, SC, T = proxies(cur, eng)
                    w = mk_waiter(env, cur, eng)

                    env.inc("dX", SY.dma_start(ident[:], id_d[:]), 16)
                    env.inc("dX", SY.dma_start(maskt[:], mask_d[:]), 16)
                    env.inc("dX", SY.dma_start(sc_sb[:], sc8_d[:]), 16)
                    env.inc("dX", SY.dma_start(b8row[:], b8_d[:]), 16)
                    k8v = k8_d.ap().rearrange("(c p) o -> c p o", p=128)
                    for c in range(8):
                        env.inc("dW", SY.dma_start(k8_sb[:, c, :], k8v[c]), 16)
                    wv_ = wl0_d.ap().rearrange("(t p) c -> t p c", p=128)
                    for t_ in range(16):
                        env.inc("dG", GP.dma_start(w_oc[:, t_, :], wv_[t_]), 16)

                    w("vector", "dX")
                    V.reciprocal(sc_tmp[:, 0:1], sc_sb[:, 0:1])
                    V.reciprocal(sc_tmp[:, 2:3], sc_sb[:, 1:2])
                    env.inc("eV", V.tensor_tensor(sc_tmp[:, 1:2], sc_sb[:, 0:1],
                                                  sc_sb[:, 1:2], ALU.mult), 1)
                    GP.memset(onesb[:], 1.0)
                    GP.memset(c001[:], 0.001)
                    env.inc("eG", GP.memset(ones_s[:], 1.0), 1)
                    V.memset(onesf[:], 1.0)
                    env.inc("eV", V.drain(), 1)
                    w("tensor", "eV")
                    T.matmul(PC[:, 0:1], onesf[:], sc_tmp[:, 0:1], start=True, stop=True)
                    T.matmul(PC[:, 1:2], onesf[:], sc_tmp[:, 1:2], start=True, stop=True)
                    env.inc("eT", T.matmul(PC[:, 2:3], onesf[:], sc_tmp[:, 2:3],
                                           start=True, stop=True), 1)
                    w("vector", "eT")
                    V.tensor_copy(rs_in[:], PC[:, 0:1])
                    V.tensor_copy(ss8[:], PC[:, 1:2])
                    env.inc("eV", V.tensor_copy(rs_k[:], PC[:, 2:3]), 1)
                    V.drain()

                    w("vector", "dW")
                    env.inc("eV", V.tensor_scalar(
                        K8Q[:].rearrange("p c o -> p (c o)"),
                        k8_sb[:].rearrange("p c o -> p (c o)"),
                        rs_k[:, 0:1], None, ALU.mult), 1)

                    w("tensor", "eG")
                    w("tensor", "eV")
                    T.matmul(PC[:], onesb[:], b8row[:, 0:512], start=True, stop=True)
                    env.inc("eT", T.matmul(PD[:], onesb[:], b8row[:, 512:1024],
                                           start=True, stop=True), 1)
                    w("vector", "eT")
                    V.tensor_copy(b8t[:, 0:512], PC[:])
                    env.inc("eV", V.tensor_copy(b8t[:, 512:1024], PD[:]), 1)

                    w("tensor", "dG")
                    evst = env.val("eV")
                    for t_ in range(16):
                        pt = PB if (t_ % 2) else PA
                        if t_ >= 2:
                            w("tensor", "eV", evst + (t_ - 1))
                        ins = None
                        for cb in range(8):
                            ins = T.transpose(pt[:, cb * 128:(cb + 1) * 128],
                                              w_oc[:, t_, cb * 128:(cb + 1) * 128],
                                              ident[:])
                        env.inc("eT", ins, 1)
                        w("vector", "eT")
                        env.inc("eV", V.tensor_copy(
                            WT[:, :, t_ * 128:(t_ + 1) * 128],
                            pt[:].rearrange("p (c o) -> p c o", o=128)), 1)

                run_phase(nc, allsems["P0"], P0)

            abes = ExitStack()
            with abes:
                SA = lambda n, sh, dt: abes.enter_context(nc.sbuf_tensor(n, sh, dt))
                K_fm = SA("K_fm", [128, 8, S], BF16)
                V_tm = SA("V_tm", [128, NT, D], BF16)
                x0t = SA("x0t", [128, D], F32)
                xn32 = SA("xn32", [128, D], F32)
                xnbf = SA("xnbf", [128, D], BF16)
                xnT = SA("xnT", [128, 8, 128], BF16)
                xq8T = SA("xq8T", [128, 8, 128], F8)
                ybf = SA("ybf", [128, D], BF16)
                stats = SA("stats", [128, 12], F32)
                mv = SA("mv", [128, 2], F32)
                rstd = SA("rstd", [128, 1], F32)
                nmr = SA("nmr", [128, 1], F32)
                negm = SA("negm", [128, 1], F32)
                denp = SA("denp", [128, 8], F32)
                den = SA("den", [128, 1], F32)
                rden = SA("rden", [128, 1], F32)
                sco = SA("sco", [128, S], BF16)
                Pb = [SA("Pb0", [128, 512], BF16), SA("Pb1", [128, 512], BF16)]
                PTs = [SA("PTs0", [128, 512], BF16), SA("PTs1", [128, 512], BF16)]
                wraw = SA("wraw", [128, 512], F32)
                qxnp = [SA("qxn0", [128, 512], BF16), SA("qxn1", [128, 512], BF16)]
                qt = SA("qt", [128, 512], BF16)
                qacc = SA("qacc", [128, 512], BF16)
                qtmp = SA("qtmp", [128, 512], BF16)
                qstage = SA("qstage", [128, 512], BF16)
                qamax = SA("qamax", [128, 32], F32)
                qscale = SA("qscale", [128, 32], F32)
                qrecip = SA("qrecip", [128, 32], F32)
                qrecipb = SA("qrecipb", [128, 32], BF16)
                qscaleb = SA("qscaleb", [128, 32], BF16)
                ABQB = {"qamax": qamax, "qscale": qscale, "qrecip": qrecip,
                        "qrecipb": qrecipb, "qscaleb": qscaleb,
                        "qxn": None, "qt": qt, "qacc": qacc, "qtmp": qtmp}

                pKV = abes.enter_context(nc.psum_tensor("pKV", [128, 512], F32))
                pKV2 = abes.enter_context(nc.psum_tensor("pKV2", [128, 512], F32))
                pT1 = abes.enter_context(nc.psum_tensor("pT1", [128, 1024], BF16))
                pT2 = abes.enter_context(nc.psum_tensor("pT2", [128, 1024], BF16))
                pF1 = abes.enter_context(nc.psum_tensor("pF1", [128, 512], F32))
                pF2 = abes.enter_context(nc.psum_tensor("pF2", [128, 512], F32))
                pS1 = abes.enter_context(nc.psum_tensor("pS1", [128, 512], F32))
                pS2 = abes.enter_context(nc.psum_tensor("pS2", [128, 512], F32))

                wq_jobs = [(wg_d, scr_wgq, 22, 2), (wu_d, scr_wuq, 22, 2),
                           (wd_d, scr_wdq, 8, 6)]

                def AB(env, cur, eng):
                    SY, GP, V, SC, T = proxies(cur, eng)
                    w = mk_waiter(env, cur, eng)
                    x0v = x0_d.ap().rearrange("(t p) c -> t p c", p=128)
                    x1v = scr_x1.ap().rearrange("(t p) c -> t p c", p=128)

                    wq_state = {"job": 0, "tile": 0, "piece": 0, "chain_ev": 0,
                                "pending": None, "tpose_ev": {}, "stage_ev": 0,
                                "par": 0}

                    def wq_chain_piece():
                        ji = wq_state["job"]
                        if ji >= len(wq_jobs):
                            return None
                        dsrc, dscr, ntile, npiece = wq_jobs[ji]
                        t_, p_ = wq_state["tile"], wq_state["piece"]
                        if p_ + 1 < npiece:
                            wq_state["piece"] += 1
                        else:
                            wq_state["piece"] = 0
                            if t_ + 1 < ntile:
                                wq_state["tile"] += 1
                            else:
                                wq_state["tile"] = 0
                                wq_state["job"] += 1
                        ncols = [D, D, 2816][ji]
                        pw = min(512, ncols - p_ * 512)
                        par = wq_state["par"]
                        wq_state["par"] = 1 - par
                        src_ = dsrc.ap()[t_ * 128:(t_ + 1) * 128, p_ * 512:p_ * 512 + pw]
                        w("sync", "eV", wq_state["chain_ev"])
                        env.inc("dW", SY.dma_start(wraw[:, 0:pw], src_), 16)
                        w("vector", "dW")
                        QB_ = dict(ABQB)
                        QB_["qxn"] = qxnp[par]
                        wq_state["chain_ev"] = fp4_chain(
                            env, V, w, wraw[:, 0:pw], pw, QB_,
                            pre_wait=lambda: w("vector", "eT",
                                               wq_state["tpose_ev"].get(par, 0)))
                        return (dscr, t_, p_, pw, par, wq_state["chain_ev"])

                    def wq_tpose_piece(info):
                        dscr, t_, p_, pw, par, chev = info
                        nb = pw // 128
                        w("tensor", "eV", chev)
                        w("tensor", "eS", wq_state.get("stage_last", 0))
                        ins = None
                        for b_ in range(nb):
                            ins = T.transpose(pT2[:, b_ * 128:(b_ + 1) * 128],
                                              qxnp[par][:, b_ * 128:(b_ + 1) * 128], ident[:])
                        wq_state["tpose_ev"][par] = env.inc("eT", ins, 1)
                        w("scalar", "eT", wq_state["tpose_ev"][par])
                        w("scalar", "dS2")
                        wq_state["stage_last"] = env.inc("eS", SC.copy(qstage[:, 0:pw], pT2[:, 0:pw]), 1)
                        SC.drain()
                        for b_ in range(nb):
                            cb = p_ * 4 + b_
                            env.inc("dS2", SC.dma_start(
                                dscr.ap()[cb, :, t_ * 128:(t_ + 1) * 128],
                                qstage[:, b_ * 128:(b_ + 1) * 128]), 16)

                    def wq_step():
                        q = wq_state.setdefault("queue", [])
                        if len(q) >= 1:
                            wq_tpose_piece(q.pop(0))
                        nxt = wq_chain_piece()
                        if nxt is not None:
                            q.append(nxt)
                        return nxt is not None or bool(q)

                    evA = {"s01": {}, "nmr": {}, "apply": {}, "xnT": {}, "x0_ld": {},
                           "store": {}}

                    def emit_stats(i):
                        w("vector", "dX", evA["x0_ld"][i])
                        w("vector", "eS", evA["apply"].get(i - 1, 0))
                        V.bn_stats(stats[:, 0:6], x0t[:, 0:512])
                        V.bn_stats(stats[:, 6:12], x0t[:, 512:1024])
                        V.drain()
                        V.bn_aggr(mv[:], stats[:])
                        V.drain()
                        env.inc("eV", V.tensor_scalar(rstd[:], mv[:, 1:2], 1e-5, None, ALU.add), 1)
                        w("scalar", "eV")
                        w("scalar", "dX", evA["x0_ld"][i])
                        env.inc("eS", SC.sqrt(rstd[:], rstd[:]), 1)
                        w("vector", "eS")
                        V.reciprocal(rstd[:], rstd[:])
                        V.drain()
                        evA["nmr"][i] = env.inc("eV", V.tensor_scalar(
                            nmr[:], mv[:, 0:1], rstd[:, 0:1], -1.0, ALU.mult, ALU.mult), 1)

                    env.inc("dX", SY.dma_start(x0t[:], x0v[0]), 16)
                    evA["x0_ld"][0] = env.val("dX")
                    emit_stats(0)

                    for i in range(NT):
                        w("scalar", "eV", evA["nmr"][i])
                        w("scalar", "dS", evA["store"].get(i - 1, 0))
                        evA["apply"][i] = env.inc("eS", SC.activation(
                            xn32[:], x0t[:], AF.Identity,
                            bias=nmr[:, 0:1], scale=rstd[:, 0:1]), 1)
                        SC.drain()
                        w("scalar", "eT", evA["xnT"].get(i - 1, 0))
                        ev_xnbf = env.inc("eS", SC.copy(xnbf[:], xn32[:]), 1)

                        w("tensor", "eS", ev_xnbf)
                        w("tensor", "eS")
                        ins = None
                        for cb in range(8):
                            ins = T.transpose(pT2[:, cb * 128:(cb + 1) * 128],
                                              xnbf[:, cb * 128:(cb + 1) * 128], ident[:])
                        evA["xnT"][i] = env.inc("eT", ins, 1)
                        w("vector", "eT", evA["xnT"][i])
                        env.inc("eV", V.tensor_copy(
                            xnT[:], pT2[:].rearrange("p (c s) -> p c s", s=128)), 1)
                        V.drain()
                        env.inc("eV", V.tensor_scalar(
                            xq8T[:].rearrange("p c s -> p (c s)"),
                            xnT[:].rearrange("p c s -> p (c s)"),
                            rs_in[:, 0:1], None, ALU.mult), 1)
                        ev_xq8 = env.val("eV")

                        kv_evac = []
                        for oc in range(8):
                            pk = pKV if (oc % 2 == 0) else pKV2
                            w("tensor", "eV", ev_xq8)
                            if oc >= 2:
                                w("tensor", "eS", kv_evac[oc - 2])
                            ins = None
                            for cc in range(8):
                                ins = T.matmul(pk[:, 0:128],
                                               WT[:, cc, oc * 128:(oc + 1) * 128],
                                               xnT[:, cc, :],
                                               start=(cc == 0), stop=(cc == 7))
                            env.inc("eT", ins, 1)
                            w("scalar", "eT")
                            kv_evac.append(env.inc("eS", SC.copy(
                                K_fm[:, oc, i * 128:(i + 1) * 128], pk[:, 0:128]), 1))
                        for half in range(2):
                            pk = pKV if (half == 0) else pKV2
                            w("tensor", "eS", kv_evac[6 + half])
                            ins = None
                            for cc in range(8):
                                ins = T.matmul(pk[:],
                                               xnT[:, cc, :],
                                               WT[:, cc, 1024 + half * 512:1024 + (half + 1) * 512],
                                               start=(cc == 0), stop=(cc == 7))
                            env.inc("eT", ins, 1)
                            w("scalar", "eT")
                            kv_evac.append(env.inc("eS", SC.copy(
                                V_tm[:, i, half * 512:(half + 1) * 512], pk[:]), 1))
                        ev_kvdone = kv_evac[-1]

                        wq_step()

                        ins = None
                        for cc in range(8):
                            T.matmul(pF1[:], xq8T[:, cc, :], K8Q[:, cc, 0:512],
                                     start=(cc == 0), stop=(cc == 7))
                            ins = T.matmul(pF2[:], xq8T[:, cc, :], K8Q[:, cc, 512:1024],
                                           start=(cc == 0), stop=(cc == 7))
                        env.inc("eT", ins, 1)
                        w("scalar", "eT")
                        SC.activation(ybf[:, 0:512], pF1[:], AF.Identity, scale=ss8[:, 0:1])
                        env.inc("eS", SC.activation(ybf[:, 512:1024], pF2[:], AF.Identity,
                                                    scale=ss8[:, 0:1]), 1)
                        ev_f8evac = env.val("eS")
                        w("vector", "eS", ev_f8evac)
                        env.inc("eV", V.tensor_tensor(ybf[:], ybf[:], b8t[:], ALU.add), 1)
                        V.drain()
                        V.tensor_tensor(xn32[:], xn32[:], x0t[:], ALU.add)
                        V.drain()
                        evA["s01"][i] = env.inc("eV", V.tensor_tensor(
                            xn32[:], xn32[:], ybf[:], ALU.add), 1)

                        if i + 1 < NT:
                            w("sync", "eV", evA["s01"][i])
                            env.inc("dX", SY.dma_start(x0t[:], x0v[i + 1]), 16)
                            evA["x0_ld"][i + 1] = env.val("dX")
                            emit_stats(i + 1)

                        wq_step()

                        nk = (i + 1) * 128
                        ngrp = (nk + 511) // 512
                        sc_evac = {}
                        pair_ev = {"eS": ev_kvdone, "eV": 0}
                        for g0 in range(0, ngrp, 2):
                            has2 = (g0 + 1 < ngrp)
                            w0 = min(512, nk - g0 * 512)
                            w1 = min(512, nk - (g0 + 1) * 512) if has2 else 0
                            w("tensor", "eV", max(pair_ev["eV"], 1) if pair_ev["eV"] else None)
                            w("tensor", "eS", max(pair_ev["eS"], ev_kvdone))
                            ins = None
                            for cc in range(8):
                                ins = T.matmul(pS1[:, 0:w0], xnT[:, cc, :],
                                               K_fm[:, cc, g0 * 512:g0 * 512 + w0],
                                               start=(cc == 0), stop=(cc == 7))
                                if has2:
                                    ins = T.matmul(pS2[:, 0:w1], xnT[:, cc, :],
                                                   K_fm[:, cc, (g0 + 1) * 512:(g0 + 1) * 512 + w1],
                                                   start=(cc == 0), stop=(cc == 7))
                            ev_smm = env.inc("eT", ins, 1)
                            for (g_, w_, pb_) in (((g0, w0, pS1),) + (((g0 + 1, w1, pS2),) if has2 else ())):
                                if g_ == ngrp - 1:
                                    w("scalar", "eT", ev_smm)
                                    ev_cp = None
                                    if w_ > 128:
                                        ev_cp = env.inc("eS", SC.copy(
                                            sco[:, g_ * 512:g_ * 512 + w_ - 128],
                                            pb_[:, 0:w_ - 128]), 1)
                                        SC.drain()
                                    w("vector", "eT", ev_smm)
                                    if ev_cp is not None:
                                        w("vector", "eS", ev_cp)
                                    ins = V.scalar_tensor_tensor(
                                        sco[:, g_ * 512 + w_ - 128:g_ * 512 + w_],
                                        pb_[:, w_ - 128:w_], 1.0, maskt[:],
                                        ALU.mult, ALU.add)
                                    pair_ev["eV"] = env.inc("eV", ins, 1)
                                else:
                                    w("scalar", "eT", ev_smm)
                                    ins = SC.copy(sco[:, g_ * 512:g_ * 512 + w_],
                                                  pb_[:, 0:w_])
                                    SC.drain()
                                    pair_ev["eS"] = env.inc("eS", ins, 1)
                        w("vector", "eS")
                        V.drain()
                        env.inc("eV", V.reduce_max(negm[:], sco[:, 0:nk], axis=AX.X,
                                                   negate=True), 1)
                        ev_negm = env.val("eV")

                        wq_step()

                        tpose_done = {}
                        pv_done = {}
                        tpose_evac = {}
                        for g in range(ngrp):
                            wg_ = min(512, nk - g * 512)
                            par = g % 2
                            ptb = pT1 if par == 0 else pT2
                            w("scalar", "eV", ev_negm)
                            if g >= 2:
                                w("scalar", "eT", tpose_done[g - 2])
                            env.inc("eS", SC.activation(Pb[par][:, 0:wg_],
                                                        sco[:, g * 512:g * 512 + wg_],
                                                        AF.Exp, bias=negm[:, 0:1], scale=1.0,
                                                        accum_out=denp[:, g:g + 1]), 1)
                            nkb = (wg_ + 127) // 128
                            w("tensor", "eS")
                            if g >= 2:
                                w("tensor", "eS", tpose_evac[g - 2])
                            ins = None
                            for kb_ in range(nkb):
                                ins = T.transpose(ptb[:, kb_ * 128:(kb_ + 1) * 128],
                                                  Pb[par][:, kb_ * 128:(kb_ + 1) * 128],
                                                  ident[:])
                            tpose_done[g] = env.inc("eT", ins, 1)
                            w("scalar", "eT", tpose_done[g])
                            if g >= 2:
                                w("scalar", "eT", pv_done[g - 2])
                            tpose_evac[g] = env.inc("eS", SC.copy(
                                PTs[par][:, 0:nkb * 128], ptb[:, 0:nkb * 128]), 1)
                            SC.drain()
                            w("tensor", "eS", tpose_evac[g])
                            ins = None
                            for kb_ in range(nkb):
                                kb = g * 4 + kb_
                                T.matmul(pF1[:], PTs[par][:, kb_ * 128:(kb_ + 1) * 128],
                                         V_tm[:, kb, 0:512],
                                         start=(kb == 0), stop=(kb == i))
                                ins = T.matmul(pF2[:], PTs[par][:, kb_ * 128:(kb_ + 1) * 128],
                                               V_tm[:, kb, 512:1024],
                                               start=(kb == 0), stop=(kb == i))
                            pv_done[g] = env.inc("eT", ins, 1)
                        w("vector", "eS")
                        V.reduce_sum(den[:], denp[:, 0:ngrp], axis=AX.X)
                        V.drain()
                        V.reciprocal(rden[:], den[:])
                        V.drain()
                        w("vector", "eT")
                        V.scalar_tensor_tensor(xn32[:, 0:512], pF1[:], rden[:, 0:1],
                                               xn32[:, 0:512], ALU.mult, ALU.add)
                        V.drain()
                        env.inc("eV", V.scalar_tensor_tensor(xn32[:, 512:1024], pF2[:],
                                                             rden[:, 0:1], xn32[:, 512:1024],
                                                             ALU.mult, ALU.add), 1)
                        w("scalar", "eV")
                        env.inc("dS", SC.dma_start(x1v[i], xn32[:]), 16)
                        evA["store"][i] = env.val("dS")

                        wq_step()

                    while wq_step():
                        pass
                    for info in wq_state.get("queue", []):
                        wq_tpose_piece(info)
                    wq_state["queue"] = []
                    w("scalar", "dS")
                    w("scalar", "dS2")
                    w("sync", "dX")
                    w("sync", "dW")
                    w("gpsimd", "dG")

                run_phase(nc, allsems["AB"], AB)

        if stop_after == "AB":
            def TAIL(env, cur, eng):
                SY, GP, V, SC, T = proxies(cur, eng)
                w = mk_waiter(env, cur, eng)
                env.inc("dX", SY.dma_start(out_d.ap(), scr_x1.ap()), 16)
                w("sync", "dX")
            run_phase(nc, allsems["TL"], TAIL)
            return nc

        l1es = ExitStack()
        with l1es:
            kv16 = l1es.enter_context(nc.sbuf_tensor("kv16", [64, 16 * 66], F32))
            kv16b = l1es.enter_context(nc.sbuf_tensor("kv16b", [64, 16 * 66], BF16))
            c1es = ExitStack()
            with c1es:
                SC1 = lambda n, sh, dt: c1es.enter_context(nc.sbuf_tensor(n, sh, dt))
                Wq_sb = SC1("Wq_sb", [128, 8, D], BF16)
                Wk_sb = SC1("Wk_sb", [128, 8, D], BF16)
                Wv_sb = SC1("Wv_sb", [128, 8, D], BF16)
                bqe = SC1("bqe", [128, 8], F32)
                g1col = SC1("g1col", [128, 8], F32)
                b1lb = SC1("b1lb", [128, 8], BF16)
                radj = SC1("radj", [128, 8], F32)
                bkrow = SC1("bkrow", [1, D], F32)
                bvrow = SC1("bvrow", [1, D], F32)
                bkrow_b = SC1("bkrow_b", [1, D], BF16)
                bvrow_b = SC1("bvrow_b", [1, D], BF16)
                x1t = SC1("x1t_c1", [128, 4, D], F32)
                xn1t = SC1("xn1t_c1", [128, 4, D], F32)
                n1b = SC1("n1b", [128, D], BF16)
                n1T = SC1("n1T", [128, 8, 512], BF16)
                phik_t = SC1("phik_t", [128, 4, D], BF16)
                v_t = SC1("v_t", [128, 4, D], BF16)
                phiq_st = [SC1("phiq_st0", [128, 512], BF16),
                           SC1("phiq_st1", [128, 512], BF16)]
                xq_st = SC1("xq_st", [128, D], BF16)
                stats = SC1("statsC", [128, 12], F32)
                mv = SC1("mvC", [128, 2], F32)
                rstd = SC1("rstdC", [128, 1], F32)
                nmr = SC1("nmrC", [128, 1], F32)
                QB = {
                    "qamax": SC1("c1qamax", [128, 64], F32),
                    "qscale": SC1("c1qscale", [128, 64], F32),
                    "qrecip": SC1("c1qrecip", [128, 64], F32),
                    "qrecipb": SC1("c1qrecipb", [128, 64], BF16),
                    "qscaleb": SC1("c1qscaleb", [128, 64], BF16),
                    "qxn": SC1("c1qxn", [128, D], BF16),
                    "qt": SC1("c1qt", [128, D], BF16),
                    "qacc": SC1("c1qacc", [128, D], BF16),
                    "qtmp": SC1("c1qtmp", [128, D], BF16),
                }
                pP = [c1es.enter_context(nc.psum_tensor("c1p%d" % b_, [128, 512], F32))
                      for b_ in range(6)]
                pTa = c1es.enter_context(nc.psum_tensor("c1ta", [128, 1024], BF16))
                pTb = c1es.enter_context(nc.psum_tensor("c1tb", [128, 1024], BF16))

                NG = S // 512

                def C1(env, cur, eng):
                    SY, GP, V, SC, T = proxies(cur, eng)
                    w = mk_waiter(env, cur, eng)
                    x1v = scr_x1.ap().rearrange("(t p) c -> t p c", p=128)
                    xn1v = scr_xn1.ap().rearrange("(t p) c -> t p c", p=128)

                    for (wd_, sb_) in ((wq_d, Wq_sb), (wk_d, Wk_sb), (wv_d, Wv_sb)):
                        wvv = wd_.ap().rearrange("(c p) o -> c p o", p=128)
                        for cc in range(8):
                            env.inc("dG", GP.dma_start(sb_[:, cc, :], wvv[cc]), 16)
                    env.inc("dW", SY.dma_start(bqe[:], bq_c_d.ap()), 16)
                    env.inc("dW", SY.dma_start(g1col[:], g1_c_d.ap()), 16)
                    env.inc("dW", SY.dma_start(radj[:], b1l_c_d.ap()), 16)
                    env.inc("dW", SY.dma_start(bkrow[:], bk_d.ap()), 16)
                    env.inc("dW", SY.dma_start(bvrow[:], bv_d.ap()), 16)
                    w("vector", "dW")
                    env.inc("eV", V.tensor_scalar(bqe[:], bqe[:], 0.001, None, ALU.add), 1)
                    V.drain()
                    env.inc("eV", V.tensor_copy(b1lb[:], radj[:]), 1)
                    V.memset(kv16[:], 0.0)
                    env.inc("eV", V.drain(), 1)
                    ev_pro = env.val("eV")

                    w("tensor", "dG")
                    w("tensor", "eV", ev_pro)
                    ins = None
                    for hc in range(8):
                        for cc in range(8):
                            ins = T.matmul(pP[hc % 4][:, hc // 4:hc // 4 + 1],
                                           Wq_sb[:, cc, hc * 128:(hc + 1) * 128],
                                           b1lb[:, cc:cc + 1],
                                           start=(cc == 0), stop=(cc == 7))
                    env.inc("eT", ins, 1)
                    w("vector", "eT")
                    for hc in range(8):
                        ins = V.tensor_copy(radj[:, hc:hc + 1], pP[hc % 4][:, hc // 4:hc // 4 + 1])
                    env.inc("eV", ins, 1)
                    V.drain()
                    env.inc("eV", V.tensor_tensor(bqe[:], bqe[:], radj[:], ALU.add), 1)
                    ev_bqe = env.val("eV")
                    w("tensor", "eV", ev_bqe)
                    ins = None
                    for bi, (sb_, half) in enumerate(((Wk_sb, 0), (Wk_sb, 1),
                                                      (Wv_sb, 0), (Wv_sb, 1))):
                        for cc in range(8):
                            ins = T.matmul(pP[bi][0:1, 0:512],
                                           b1lb[:, cc:cc + 1],
                                           sb_[:, cc, half * 512:(half + 1) * 512],
                                           start=(cc == 0), stop=(cc == 7))
                    env.inc("eT", ins, 1)
                    w("vector", "eT")
                    V.tensor_tensor(bkrow[0:1, 0:512], bkrow[0:1, 0:512], pP[0][0:1, 0:512], ALU.add)
                    V.tensor_tensor(bkrow[0:1, 512:1024], bkrow[0:1, 512:1024], pP[1][0:1, 0:512], ALU.add)
                    V.tensor_tensor(bvrow[0:1, 0:512], bvrow[0:1, 0:512], pP[2][0:1, 0:512], ALU.add)
                    env.inc("eV", V.tensor_tensor(bvrow[0:1, 512:1024], bvrow[0:1, 512:1024],
                                                  pP[3][0:1, 0:512], ALU.add), 1)
                    V.drain()
                    V.tensor_copy(bkrow_b[:], bkrow[:])
                    env.inc("eV", V.tensor_copy(bvrow_b[:], bvrow[:]), 1)
                    V.drain()
                    for sb_ in (Wq_sb, Wk_sb, Wv_sb):
                        for cc in range(8):
                            ins = V.tensor_scalar(sb_[:, cc, :], sb_[:, cc, :],
                                                  g1col[:, cc:cc + 1], None, ALU.mult)
                    env.inc("eV", ins, 1)
                    V.drain()
                    ev_wscaled = env.val("eV")

                    last_xq_tp = [0]
                    pq_store = {}
                    bankev = {}
                    evC = {"n1T": {}, "chain": {}, "qevac": {}, "kvevac": {}, "phv": {},
                           "kvadd": {}, "xqT": {}, "xqS": {}, "ln": {}}
                    for j in range(NG):
                        w("sync", "eS", evC["ln"].get(j - 1, 0))
                        for st in range(4):
                            env.inc("dX", SY.dma_start(x1t[:, st, :], x1v[4 * j + st]), 16)
                        ld_now = env.val("dX")

                        for st in range(4):
                            w("vector", "dX", ld_now)
                            w("scalar", "dX", ld_now)
                            emit_ln(env, V, SC, w, x1t[:, st, :], xn1t[:, st, :],
                                    stats, mv, rstd, nmr, 1e-5)
                            SC.drain()
                            env.inc("dS", SC.dma_start(xn1v[4 * j + st], xn1t[:, st, :]), 16)
                            w("vector", "eS")
                            emit_ln(env, V, SC, w, xn1t[:, st, :], n1b[:],
                                    stats, mv, rstd, nmr, 1e-6)
                            ev_n1 = env.val("eS")
                            evC["ln"][j] = ev_n1
                            w("tensor", "eS", ev_n1)
                            bname = "pTa" if st % 2 == 0 else "pTb"
                            w("tensor", "eS", bankev.get(bname, 0))
                            ptb = pTa if st % 2 == 0 else pTb
                            ins = None
                            for cb in range(8):
                                ins = T.transpose(ptb[:, cb * 128:(cb + 1) * 128],
                                                  n1b[:, cb * 128:(cb + 1) * 128], ident[:])
                            env.inc("eT", ins, 1)
                            w("scalar", "eT")
                            evC["n1T"][(j, st)] = env.inc("eS", SC.copy(
                                n1T[:, :, st * 128:(st + 1) * 128],
                                ptb[:].rearrange("p (c s) -> p c s", s=128)), 1)
                            bankev[bname] = evC["n1T"][(j, st)]
                        ev_n1T_all = evC["n1T"][(j, 3)]

                        for st in range(4):
                            ch = fp4_chain(env, V, w, xn1t[:, st, :], D, QB,
                                           pre_wait=lambda: w("vector", "eT", last_xq_tp[0]))
                            evC["chain"][(j, st)] = ch
                            w("tensor", "eV", ch)
                            bname = "pTb" if st % 2 == 0 else "pTa"
                            w("tensor", "eS", bankev.get(bname, 0))
                            ptq = pTb if st % 2 == 0 else pTa
                            ins = None
                            for cb in range(8):
                                ins = T.transpose(ptq[:, cb * 128:(cb + 1) * 128],
                                                  QB["qxn"][:, cb * 128:(cb + 1) * 128], ident[:])
                            last_xq_tp[0] = env.inc("eT", ins, 1)
                            w("scalar", "eT", last_xq_tp[0])
                            w("scalar", "dW2")
                            evC["xqS"][(j, st)] = env.inc("eS", SC.copy(xq_st[:], ptq[:]), 1)
                            bankev[bname] = evC["xqS"][(j, st)]
                            SC.drain()
                            for cb in range(8):
                                env.inc("dW2", SC.dma_start(
                                    scr_xq4T.ap()[cb, :, (4 * j + st) * 128:(4 * j + st + 1) * 128],
                                    xq_st[:, cb * 128:(cb + 1) * 128]), 16)

                        for hc in range(8):
                            pq = pP[hc % 2]
                            qpar = hc % 2
                            w("tensor", "eS", ev_n1T_all)
                            w("tensor", "eS", bankev.get("pP%d" % (hc % 2), 0))
                            if hc < 2:
                                w("tensor", "eV", evC["kvadd"].get((j - 1, 2), 0))
                            ins = None
                            for cc in range(8):
                                ins = T.matmul(pq[:], Wq_sb[:, cc, hc * 128:(hc + 1) * 128],
                                               n1T[:, cc, :], start=(cc == 0), stop=(cc == 7))
                            env.inc("eT", ins, 1)
                            qsem = "dS2" if qpar == 0 else "dS3"
                            w("scalar", "eT")
                            w("scalar", qsem, pq_store.get((j, hc - 2),
                                                           pq_store.get((j - 1, 6 + qpar), 0)))
                            SC.activation(phiq_st[qpar][:], pq[:], AF.Relu,
                                          bias=bqe[:, hc:hc + 1], scale=1.0)
                            SC.drain()
                            bankev["pP%d" % (hc % 2)] = env.inc("eS", SC.add(
                                phiq_st[qpar][:], phiq_st[qpar][:], c001[:, 0:1]), 1)
                            SC.drain()
                            env.inc(qsem, SC.dma_start(
                                scr_phiqT.ap()[:, 2 * hc, j * 512:(j + 1) * 512],
                                phiq_st[qpar][0:64, :]), 16)
                            pq_store[(j, hc)] = env.inc(qsem, SC.dma_start(
                                scr_phiqT.ap()[:, 2 * hc + 1, j * 512:(j + 1) * 512],
                                phiq_st[qpar][64:128, :]), 16)

                        for st in range(4):
                            for (sb_, brow, dst, isk) in ((Wk_sb, bkrow_b, phik_t, True),
                                                          (Wv_sb, bvrow_b, v_t, False)):
                                for half in range(2):
                                    pq = pP[2 + half]
                                    w("tensor", "eS", bankev.get("pP%d" % (2 + half), 0))
                                    if st == 0 and isk:
                                        w("tensor", "eV", evC["kvadd"].get((j - 1, 2), 0))
                                    for cc in range(8):
                                        T.matmul(pq[:], n1T[:, cc, st * 128:(st + 1) * 128],
                                                 sb_[:, cc, half * 512:(half + 1) * 512],
                                                 start=(cc == 0), stop=False)
                                    ins = T.matmul(pq[:], onesb[:],
                                                   brow[0:1, half * 512:(half + 1) * 512],
                                                   start=False, stop=True)
                                    env.inc("eT", ins, 1)
                                    w("scalar", "eT")
                                    if isk:
                                        SC.activation(dst[:, st, half * 512:(half + 1) * 512],
                                                      pq[:], AF.Relu, bias=c001[:, 0:1], scale=1.0)
                                        SC.drain()
                                        ins = SC.add(dst[:, st, half * 512:(half + 1) * 512],
                                                     dst[:, st, half * 512:(half + 1) * 512],
                                                     c001[:, 0:1])
                                    else:
                                        ins = SC.copy(dst[:, st, half * 512:(half + 1) * 512], pq[:])
                                    bankev["pP%d" % (2 + half)] = env.inc("eS", ins, 1)
                        ev_phiv = env.val("eS")

                        for gidx, grp in enumerate(((0, 6), (6, 12), (12, 16))):
                            nh = grp[1] - grp[0]
                            pb_ = pP[gidx]
                            w("tensor", "eS", ev_phiv)
                            w("tensor", "eV", evC["kvadd"].get((j, gidx - 1),
                                                               evC["kvadd"].get((j - 1, 2), 0)))
                            ins = None
                            for hi, h in enumerate(range(*grp)):
                                for st in range(4):
                                    T.matmul(pb_[0:64, hi * 66:hi * 66 + 64],
                                             phik_t[:, st, h * 64:(h + 1) * 64],
                                             v_t[:, st, h * 64:(h + 1) * 64],
                                             start=(hi == 0 and st == 0), stop=False)
                                    ins = T.matmul(pb_[0:64, hi * 66 + 64:hi * 66 + 65],
                                                   phik_t[:, st, h * 64:(h + 1) * 64],
                                                   ones_s[:],
                                                   start=False,
                                                   stop=(hi == nh - 1 and st == 3))
                            env.inc("eT", ins, 1)
                            w("vector", "eT")
                            ins = V.tensor_tensor(
                                kv16[0:64, grp[0] * 66:grp[0] * 66 + nh * 66],
                                kv16[0:64, grp[0] * 66:grp[0] * 66 + nh * 66],
                                pb_[0:64, 0:nh * 66], ALU.add)
                            evC["kvadd"][(j, gidx)] = env.inc("eV", ins, 1)
                            V.drain()

                    env.inc("eV", V.tensor_copy(kv16b[:], kv16[:]), 1)
                    w("scalar", "dS")
                    w("scalar", "dS2")
                    w("scalar", "dS3")
                    w("scalar", "dW2")
                    w("sync", "dX")
                    w("gpsimd", "dG")

                run_phase(nc, allsems["C1"], C1)

            if stop_after == "C1":
                def TAILC(env, cur, eng):
                    SY, GP, V, SC, T = proxies(cur, eng)
                    w = mk_waiter(env, cur, eng)
                    env.inc("dX", SY.dma_start(out_d.ap(), scr_xn1.ap()), 16)
                    w("sync", "dX")
                run_phase(nc, allsems["TL"], TAILC)
                return nc

            c2es = ExitStack()
            with c2es:
                SC2 = lambda n, sh, dt: c2es.enter_context(nc.sbuf_tensor(n, sh, dt))
                wgq_sb = SC2("wgq_sb", [128, 8, 2816], BF16)
                wuq_sb = SC2("wuq_sb", [128, 8, 2816], BF16)
                wdq_sb = SC2("wdq_sb", [128, 22, D], BF16)
                x1t2 = SC2("x1t2", [128, D], F32)
                xq4t = [SC2("xq4t0", [128, 8, 128], BF16), SC2("xq4t1", [128, 8, 128], BF16)]
                gs = SC2("gs", [128, 2816], BF16)
                us_ = SC2("us_", [128, 2816], BF16)
                hh = [SC2("hh0", [128, 2816], BF16), SC2("hh1", [128, 2816], BF16)]
                hqT = SC2("hqT", [128, 22, 128], BF16)
                p1t = SC2("p1t", [128, D], F32)
                QB2 = {
                    "qamax": SC2("c2qamax", [128, 176], F32),
                    "qscale": SC2("c2qscale", [128, 176], F32),
                    "qrecip": SC2("c2qrecip", [128, 176], F32),
                    "qrecipb": SC2("c2qrecipb", [128, 176], BF16),
                    "qscaleb": SC2("c2qscaleb", [128, 176], BF16),
                    "qxn": SC2("c2qxn", [128, 2816], BF16),
                    "qt": SC2("c2qt", [128, 2816], BF16),
                    "qacc": SC2("c2qacc", [128, 2816], BF16),
                    "qtmp": SC2("c2qtmp", [128, 2816], BF16),
                }
                pQ = [c2es.enter_context(nc.psum_tensor("c2p%d" % b_, [128, 512], F32))
                      for b_ in range(6)]
                pTa2 = c2es.enter_context(nc.psum_tensor("c2ta", [128, 1024], BF16))
                pTb2 = c2es.enter_context(nc.psum_tensor("c2tb", [128, 1024], BF16))

                OCH = [(0, 512), (512, 1024), (1024, 1536), (1536, 2048),
                       (2048, 2560), (2560, 2816)]

                def C2(env, cur, eng):
                    SY, GP, V, SC, T = proxies(cur, eng)
                    w = mk_waiter(env, cur, eng)
                    x1v = scr_x1.ap().rearrange("(t p) c -> t p c", p=128)
                    p1v = scr_p1.ap().rearrange("(t p) c -> t p c", p=128)

                    for cc in range(8):
                        env.inc("dW", SY.dma_start(wgq_sb[:, cc, :], scr_wgq.ap()[cc]), 16)
                    for cc in range(8):
                        env.inc("dW", SY.dma_start(wuq_sb[:, cc, :], scr_wuq.ap()[cc]), 16)
                    for kk in range(22):
                        env.inc("dW", SY.dma_start(wdq_sb[:, kk, :], scr_wdq.ap()[kk]), 16)
                    ld_w = env.val("dW")

                    NST = S // 128
                    ev = {"hmul": {}, "chain": {}, "hqT_Tg": {}, "hqT_S": {},
                          "down": {}, "p1": {}, "gu": {}, "silu": {}, "xq_ld": {},
                          "x1_ld": {}}

                    def stageA(st):
                        par = st % 2
                        xq = xq4t[par]
                        for p_ in range(6):
                            o0, o1 = OCH[p_]
                            bg = pQ[p_ % 2]
                            bu = pQ[2 + p_ % 2]
                            w("tensor", "dX", ev["xq_ld"][st])
                            w("tensor", "dW", ld_w)
                            prev = (st, p_ - 2) if p_ >= 2 else (st - 1, 4 + p_)
                            w("tensor", "eS", ev["silu"].get(prev, 0))
                            w("tensor", "eG", ev["hmul"].get(prev, 0))
                            ins = None
                            for cc in range(8):
                                T.matmul(bg[:, 0:o1 - o0], xq[:, cc, :],
                                         wgq_sb[:, cc, o0:o1],
                                         start=(cc == 0), stop=(cc == 7))
                                ins = T.matmul(bu[:, 0:o1 - o0], xq[:, cc, :],
                                               wuq_sb[:, cc, o0:o1],
                                               start=(cc == 0), stop=(cc == 7))
                            ev["gu"][(st, p_)] = env.inc("eT", ins, 1)
                            w("scalar", "eT", ev["gu"][(st, p_)])
                            w("scalar", "eG", ev["hmul"].get(prev, 0))
                            SC.activation(gs[:, o0:o1], bg[:, 0:o1 - o0], AF.Silu)
                            SC.drain()
                            ins = SC.copy(us_[:, o0:o1], bu[:, 0:o1 - o0])
                            SC.drain()
                            ev["silu"][(st, p_)] = env.inc("eS", ins, 1)
                            w("gpsimd", "eS", ev["silu"][(st, p_)])
                            hpar = hh[par]
                            ins = GP.tensor_tensor(hpar[:, o0:o1], gs[:, o0:o1],
                                                   us_[:, o0:o1], ALU.mult)
                            GP.drain()
                            ev["hmul"][(st, p_)] = env.inc("eG", ins, 1)
                        if st + 1 < NST:
                            w("sync", "eT", ev["gu"][(st, 5)])
                            for cb in range(8):
                                env.inc("dX", SY.dma_start(
                                    xq4t[(st + 1) % 2][:, cb, :],
                                    scr_xq4T.ap()[cb, :, (st + 1) * 128:(st + 2) * 128]), 16)
                            ev["xq_ld"][st + 1] = env.val("dX")

                    def stageB_V(st):
                        par = st % 2
                        chs = []
                        for j, (c0, c1) in enumerate(((0, 1024), (1024, 2048),
                                                      (2048, 2816))):
                            b0 = c0 // 16
                            Bj = {k: (v[:, b0:176] if k in
                                      ("qamax", "qscale", "qrecip", "qrecipb",
                                       "qscaleb") else v[:, c0:c1])
                                  for k, v in QB2.items()}
                            def _pre(j=j):
                                w("vector", "eT", ev["hqT_Tg"].get((st - 1, j), 0))
                                w("vector", "eG", ev["hmul"][(st, 2 * j + 1)])
                            chs.append(fp4_chain(env, V, w, hh[par][:, c0:c1],
                                                 c1 - c0, Bj, pre_wait=_pre))
                        ev["chain"][st] = chs

                    def stageB(st):
                        chs = ev["chain"][st]
                        done = 0
                        gi = 0
                        prev_same_bank = {0: (st - 1, 2), 1: (st - 1, 1), 2: (st, 0)}
                        for grp in (8, 8, 6):
                            ptb = pTa2 if gi % 2 == 0 else pTb2
                            w("tensor", "eV", chs[gi])
                            w("tensor", "eS", ev["hqT_S"].get(prev_same_bank[gi], 0))
                            ins = None
                            for b_ in range(grp):
                                ins = T.transpose(ptb[:, b_ * 128:(b_ + 1) * 128],
                                                  QB2["qxn"][:, (done + b_) * 128:(done + b_ + 1) * 128],
                                                  ident[:])
                            ev["hqT_Tg"][(st, gi)] = env.inc("eT", ins, 1)
                            w("scalar", "eT", ev["hqT_Tg"][(st, gi)])
                            ev["hqT_S"][(st, gi)] = env.inc("eS", SC.copy(
                                hqT[:, done:done + grp, :].rearrange("p k s -> p (k s)"),
                                ptb[:, 0:grp * 128]), 1)
                            done += grp
                            gi += 1
                        w("sync", "eV", ev["p1"].get(st - 1, 0))
                        env.inc("dW", SY.dma_start(x1t2[:], x1v[st]), 16)
                        ev["x1_ld"][st] = env.val("dW")
                        w("tensor", "eS", ev["hqT_S"][(st, 2)])
                        w("tensor", "eV", ev["p1"].get(st - 1, 0))
                        ins = None
                        for kk in range(22):
                            T.matmul(pQ[4][:], hqT[:, kk, :], wdq_sb[:, kk, 0:512],
                                     start=(kk == 0), stop=(kk == 21))
                            ins = T.matmul(pQ[5][:], hqT[:, kk, :], wdq_sb[:, kk, 512:1024],
                                           start=(kk == 0), stop=(kk == 21))
                        ev["down"][st] = env.inc("eT", ins, 1)
                        w("vector", "eT", ev["down"][st])
                        w("vector", "dW", ev["x1_ld"][st])
                        w("vector", "dS")
                        V.tensor_tensor(p1t[:, 0:512], x1t2[:, 0:512], pQ[4][:], ALU.add)
                        ev["p1"][st] = env.inc("eV", V.tensor_tensor(
                            p1t[:, 512:1024], x1t2[:, 512:1024], pQ[5][:], ALU.add), 1)
                        w("scalar", "eV", ev["p1"][st])
                        env.inc("dS", SC.dma_start(p1v[st], p1t[:]), 16)

                    for cb in range(8):
                        env.inc("dX", SY.dma_start(xq4t[0][:, cb, :],
                                                   scr_xq4T.ap()[cb, :, 0:128]), 16)
                    ev["xq_ld"][0] = env.val("dX")

                    for st in range(NST):
                        if st > 0:
                            stageB_V(st - 1)
                        stageA(st)
                        if st > 0:
                            stageB(st - 1)
                    stageB_V(NST - 1)
                    stageB(NST - 1)

                    w("scalar", "dS")
                    w("sync", "dX")
                    w("sync", "dW")

                run_phase(nc, allsems["C2"], C2)

            if stop_after == "C2":
                def TAILC2(env, cur, eng):
                    SY, GP, V, SC, T = proxies(cur, eng)
                    w = mk_waiter(env, cur, eng)
                    env.inc("dX", SY.dma_start(out_d.ap(), scr_p1.ap()), 16)
                    w("sync", "dX")
                run_phase(nc, allsems["TL"], TAILC2)
                return nc

            des = ExitStack()
            with des:
                SD = lambda n, sh, dt: des.enter_context(nc.sbuf_tensor(n, sh, dt))
                W1_sb = SD("W1_sb", [128, 8, 4096], BF16)
                W2_sb = SD("W2_sb", [128, 32, D], BF16)
                g2col = SD("g2col", [128, 8], F32)
                b2lb = SD("b2lb", [128, 8], BF16)

                d1es = ExitStack()
                with d1es:
                    SD1 = lambda n, sh, dt: d1es.enter_context(nc.sbuf_tensor(n, sh, dt))
                    Wo_sb = SD1("Wo_sb", [128, 8, D], BF16)
                    borowb = SD1("borowb", [1, D], BF16)
                    borow = SD1("borow", [1, D], F32)
                    pqT = [SD1("pqT0", [64, 16, 128], BF16),
                           SD1("pqT1", [64, 16, 128], BF16)]
                    xn1t3 = SD1("xn1t3", [128, D], F32)
                    p1t3 = SD1("p1t3", [128, D], F32)
                    att = SD1("att", [128, D], BF16)
                    attT = SD1("attT", [128, 8, 128], BF16)
                    xrt = SD1("xrt", [128, D], F32)
                    n2b = SD1("n2b", [128, D], BF16)
                    n2st = SD1("n2st", [128, D], BF16)
                    s2t = SD1("s2t", [128, D], F32)
                    dent = SD1("dent", [128, 16], F32)
                    rden3 = SD1("rden3", [128, 16], F32)
                    stats3 = SD1("stats3", [128, 12], F32)
                    mv3 = SD1("mv3", [128, 2], F32)
                    rstd3 = SD1("rstd3", [128, 1], F32)
                    nmr3 = SD1("nmr3", [128, 1], F32)
                    pN = [d1es.enter_context(nc.psum_tensor("d1n%d" % b_, [128, 512], F32))
                          for b_ in range(5)]
                    pTa3 = d1es.enter_context(nc.psum_tensor("d1ta", [128, 1024], BF16))
                    pTb3 = d1es.enter_context(nc.psum_tensor("d1tb", [128, 1024], BF16))
                    HB = [(h // 7, (h % 7) * 66) if h < 14 else (2, (h - 14) * 66)
                          for h in range(16)]

                    def D1(env, cur, eng):
                        SY, GP, V, SC, T = proxies(cur, eng)
                        w = mk_waiter(env, cur, eng)
                        xn1v = scr_xn1.ap().rearrange("(t p) c -> t p c", p=128)
                        p1v = scr_p1.ap().rearrange("(t p) c -> t p c", p=128)
                        s2v = scr_s2.ap().rearrange("(t p) c -> t p c", p=128)

                        wov = wo_d.ap().rearrange("(c p) o -> c p o", p=128)
                        for cc in range(8):
                            env.inc("dG", GP.dma_start(Wo_sb[:, cc, :], wov[cc]), 16)
                        w1v = w1_d.ap().rearrange("(c p) o -> c p o", p=128)
                        for cc in range(8):
                            env.inc("dG2", GP.dma_start(W1_sb[:, cc, :], w1v[cc]), 16)
                        w2v = w2_d.ap().rearrange("(c p) o -> c p o", p=128)
                        for kk in range(32):
                            env.inc("dG2", GP.dma_start(W2_sb[:, kk, :], w2v[kk]), 16)
                        env.inc("dS2", SC.dma_start(borow[:], bo_d.ap()), 16)
                        env.inc("dS2", SC.dma_start(g2col[:], g2_c_d.ap()), 16)
                        env.inc("dS2", SC.dma_start(dent[:, 0:8], b2l_c_d.ap()), 16)
                        w("vector", "dS2")
                        V.tensor_copy(borowb[:], borow[:])
                        env.inc("eV", V.tensor_copy(b2lb[:], dent[:, 0:8]), 1)
                        V.drain()

                        NTL = S // 128
                        ev = {"num": {}, "att": {}, "xr": {}, "s2": {}, "n2T": {},
                              "pq_ld": {}, "xn_ld": {}, "p1_ld": {}, "n2st": {}}
                        env.inc("dX", SY.dma_start(pqT[0][0:64, 0:8, :],
                                                   scr_phiqT.ap()[:, 0:8, 0:128]), 16)
                        env.inc("dX", SY.dma_start(pqT[0][0:64, 8:16, :],
                                                   scr_phiqT.ap()[:, 8:16, 0:128]), 16)
                        ev["pq_ld"][0] = ("dX", env.val("dX"))

                        def emit_n2T(t):
                            w("tensor", "eS", ev["n2act"][t])
                            ptb2 = pTb3 if t % 2 == 0 else pTa3
                            ins = None
                            for cb in range(8):
                                ins = T.transpose(ptb2[:, cb * 128:(cb + 1) * 128],
                                                  n2b[:, cb * 128:(cb + 1) * 128], ident[:])
                            ev["n2T"][t] = env.inc("eT", ins, 1)
                            w("scalar", "eT")
                            w("scalar", "dS")
                            ev["n2st"][t] = env.inc("eS", SC.copy(n2st[:], ptb2[:]), 1)
                            SC.drain()
                            for cb in range(8):
                                env.inc("dS", SC.dma_start(
                                    scr_y0T.ap()[cb, :, t * 128:(t + 1) * 128],
                                    n2st[:, cb * 128:(cb + 1) * 128]), 16)

                        for i in range(NTL):
                            par = i % 2
                            w("sync", "eV", ev["xr"].get(i - 1, 0))
                            env.inc("dW", SY.dma_start(xn1t3[:], xn1v[i]), 16)
                            ev["xn_ld"][i] = env.val("dW")
                            w("sync", "eV", ev["s2"].get(i - 1, 0))
                            env.inc("dW2", SY.dma_start(p1t3[:], p1v[i]), 16)
                            ev["p1_ld"][i] = env.val("dW2")
                            if i + 1 < NTL:
                                sm = "dX" if (i + 1) % 2 == 0 else "dX2"
                                w("sync", "eT", ev["num"].get(i - 1, 0))
                                env.inc(sm, SY.dma_start(
                                    pqT[1 - par][0:64, 0:8, :],
                                    scr_phiqT.ap()[:, 0:8, (i + 1) * 128:(i + 2) * 128]), 16)
                                env.inc(sm, SY.dma_start(
                                    pqT[1 - par][0:64, 8:16, :],
                                    scr_phiqT.ap()[:, 8:16, (i + 1) * 128:(i + 2) * 128]), 16)
                                ev["pq_ld"][i + 1] = (sm, env.val(sm))

                            w("tensor", ev["pq_ld"][i][0], ev["pq_ld"][i][1])
                            w("tensor", "eV", ev["att"].get(i - 1, 0))
                            ins = None
                            for h in range(16):
                                b_, off = HB[h]
                                ins = T.matmul(pN[b_][:, off:off + 65],
                                               pqT[par][0:64, h, :],
                                               kv16b[0:64, h * 66:h * 66 + 65],
                                               start=True, stop=True)
                            ev["num"][i] = env.inc("eT", ins, 1)
                            w("vector", "eT", ev["num"][i])
                            for (b_, h0, nh) in ((0, 0, 7), (1, 7, 7), (2, 14, 2)):
                                ins = V.tensor_copy(
                                    dent[:, h0:h0 + nh],
                                    bass.AP(pN[b_], 64, [[512, 128], [66, nh], [1, 1]]))
                            env.inc("eV", ins, 1)
                            V.drain()
                            env.inc("eV", V.reciprocal(rden3[:], dent[:]), 1)
                            V.drain()
                            for (b_, h0, nh) in ((0, 0, 7), (1, 7, 7), (2, 14, 2)):
                                ins = V.tensor_tensor(
                                    att[:, h0 * 64:(h0 + nh) * 64].rearrange(
                                        "p (h x) -> p h x", x=64),
                                    bass.AP(pN[b_], 0, [[512, 128], [66, nh], [1, 64]]),
                                    rden3[:, h0:h0 + nh].unsqueeze(-1).broadcast_to((128, nh, 64)),
                                    ALU.mult)
                            ev["att"][i] = env.inc("eV", ins, 1)

                            w("tensor", "eV", ev["att"][i])
                            w("tensor", "eS", ev["n2st"].get(i - 3, 0))
                            ptb = pTa3 if par == 0 else pTb3
                            ins = None
                            for cb in range(8):
                                ins = T.transpose(ptb[:, cb * 128:(cb + 1) * 128],
                                                  att[:, cb * 128:(cb + 1) * 128], ident[:])
                            env.inc("eT", ins, 1)
                            w("vector", "eT")
                            env.inc("eV", V.tensor_copy(
                                attT[:], ptb[:].rearrange("p (c s) -> p c s", s=128)), 1)
                            ev_attT = env.val("eV")

                            w("tensor", "eV", ev_attT)
                            w("tensor", "dG", 16 * 8)
                            for half in range(2):
                                for cc in range(8):
                                    T.matmul(pN[3 + half][:], attT[:, cc, :],
                                             Wo_sb[:, cc, half * 512:(half + 1) * 512],
                                             start=(cc == 0), stop=False)
                                ins = T.matmul(pN[3 + half][:], onesb[:],
                                               borowb[0:1, half * 512:(half + 1) * 512],
                                               start=False, stop=True)
                            ev_wo = env.inc("eT", ins, 1)
                            if i >= 1:
                                emit_n2T(i - 1)
                            w("vector", "eT", ev_wo)
                            w("vector", "dW", ev["xn_ld"][i])
                            w("vector", "eS", ev.get("n2act", {}).get(i - 1, 0))
                            V.tensor_tensor(xrt[:, 0:512], xn1t3[:, 0:512], pN[3][:], ALU.add)
                            ev["xr"][i] = env.inc("eV", V.tensor_tensor(
                                xrt[:, 512:1024], xn1t3[:, 512:1024], pN[4][:], ALU.add), 1)
                            V.drain()
                            w("scalar", "eT", ev["n2T"].get(i - 1, 0))
                            emit_ln(env, V, SC, w, xrt[:], n2b[:], stats3, mv3, rstd3, nmr3, 1e-6)
                            ev_n2 = env.val("eS")
                            ev.setdefault("n2act", {})[i] = ev_n2
                            w("vector", "dW2", ev["p1_ld"][i])
                            w("vector", "dS")
                            ev["s2"][i] = env.inc("eV", V.tensor_tensor(
                                s2t[:], p1t3[:], xrt[:], ALU.add), 1)
                            w("scalar", "eV", ev["s2"][i])
                            env.inc("dS", SC.dma_start(s2v[i], s2t[:]), 16)

                        emit_n2T(NTL - 1)
                        w("scalar", "dS")
                        w("sync", "dX")
                        w("sync", "dW")
                        w("gpsimd", "dG")
                        w("gpsimd", "dG2")

                    run_phase(nc, allsems["D1"], D1)

                d2es = ExitStack()
                with d2es:
                    SD2 = lambda n, sh, dt: d2es.enter_context(nc.sbuf_tensor(n, sh, dt))
                    y0T_t = [SD2("y0T_t0", [128, 8, 128], BF16),
                             SD2("y0T_t1", [128, 8, 128], BF16)]
                    s2t2 = SD2("s2t2", [128, D], F32)
                    y1b = SD2("y1b", [128, 4096], BF16)
                    y1T_t = SD2("y1T_t", [128, 32, 128], BF16)
                    outt = SD2("outt", [128, D], F32)
                    b2row_f = SD2("b2row_f", [1, D], F32)
                    b1row = SD2("b1row", [1, 4096], F32)
                    b1rowb = SD2("b1rowb", [1, 4096], BF16)
                    b2rowb = SD2("b2rowb", [1, D], BF16)
                    pE = [d2es.enter_context(nc.psum_tensor("d2p%d" % b_, [128, 512], F32))
                          for b_ in range(4)]
                    pTa4 = d2es.enter_context(nc.psum_tensor("d2ta", [128, 1024], BF16))
                    pTb4 = d2es.enter_context(nc.psum_tensor("d2tb", [128, 1024], BF16))

                    def D2(env, cur, eng):
                        SY, GP, V, SC, T = proxies(cur, eng)
                        w = mk_waiter(env, cur, eng)
                        y0v = scr_y0T.ap()
                        s2v = scr_s2.ap().rearrange("(t p) c -> t p c", p=128)
                        outv = out_d.ap().rearrange("(t p) c -> t p c", p=128)

                        env.inc("dS2", SC.dma_start(b1row[:], b1_d.ap()), 16)
                        env.inc("dS2", SC.dma_start(b2row_f[:], b2_d.ap()), 16)
                        ins = None
                        for fch in range(8):
                            w("tensor", "eV")
                            for cc in range(8):
                                ins = T.matmul(pE[fch % 2][0:1, 0:512],
                                               b2lb[:, cc:cc + 1],
                                               W1_sb[:, cc, fch * 512:(fch + 1) * 512],
                                               start=(cc == 0), stop=(cc == 7))
                            env.inc("eT", ins, 1)
                            w("vector", "eT")
                            w("vector", "dS2")
                            env.inc("eV", V.tensor_tensor(
                                b1row[0:1, fch * 512:(fch + 1) * 512],
                                b1row[0:1, fch * 512:(fch + 1) * 512],
                                pE[fch % 2][0:1, 0:512], ALU.add), 1)
                        V.drain()
                        V.tensor_copy(b1rowb[:], b1row[:])
                        env.inc("eV", V.tensor_copy(b2rowb[:], b2row_f[:]), 1)
                        V.drain()
                        for cc in range(8):
                            ins = V.tensor_scalar(W1_sb[:, cc, :], W1_sb[:, cc, :],
                                                  g2col[:, cc:cc + 1], None, ALU.mult)
                        env.inc("eV", ins, 1)
                        V.drain()
                        ev_w1 = env.val("eV")

                        NTL = S // 128
                        ev = {"w1": {}, "gelu": {}, "y1T": {}, "w2": {}, "out": {},
                              "y0_ld": {}, "s2_ld": {}}
                        for cb in range(8):
                            env.inc("dX", SY.dma_start(y0T_t[0][:, cb, :],
                                                       y0v[cb, :, 0:128]), 16)
                        ev["y0_ld"][0] = env.val("dX")

                        for i in range(NTL):
                            par = i % 2
                            w("sync", "eV", ev["out"].get(i - 1, 0))
                            env.inc("dW", SY.dma_start(s2t2[:], s2v[i]), 16)
                            ev["s2_ld"][i] = env.val("dW")

                            for fch in range(8):
                                pb_ = pE[fch % 2]
                                w("tensor", "dX", ev["y0_ld"][i])
                                w("tensor", "eV", ev_w1)
                                w("tensor", "eS", ev["gelu"].get((i, fch - 2),
                                                                 ev["gelu"].get((i - 1, 6 + (fch % 2)), 0)))
                                for cc in range(8):
                                    T.matmul(pb_[:], y0T_t[par][:, cc, :],
                                             W1_sb[:, cc, fch * 512:(fch + 1) * 512],
                                             start=(cc == 0), stop=False)
                                ins = T.matmul(pb_[:], onesb[:],
                                               b1rowb[0:1, fch * 512:(fch + 1) * 512],
                                               start=False, stop=True)
                                ev["w1"][(i, fch)] = env.inc("eT", ins, 1)
                                w("scalar", "eT", ev["w1"][(i, fch)])
                                ev["gelu"][(i, fch)] = env.inc("eS", SC.activation(
                                    y1b[:, fch * 512:(fch + 1) * 512],
                                    pb_[:], AF.Gelu_apprx_tanh), 1)
                            if i + 1 < NTL:
                                w("sync", "eT", ev["w1"][(i, 7)])
                                for cb in range(8):
                                    env.inc("dX", SY.dma_start(
                                        y0T_t[1 - par][:, cb, :],
                                        y0v[cb, :, (i + 1) * 128:(i + 2) * 128]), 16)
                                ev["y0_ld"][i + 1] = env.val("dX")

                            for grp in range(4):
                                ptb = pTa4 if grp % 2 == 0 else pTb4
                                w("tensor", "eS", ev["gelu"][(i, 2 * grp + 1)])
                                w("tensor", "eV", ev["y1T"].get((i, grp - 2),
                                                                ev["y1T"].get((i - 1, grp + 2 - 4 * (grp >= 2)), 0)))
                                ins = None
                                for b_ in range(8):
                                    blk = grp * 8 + b_
                                    ins = T.transpose(ptb[:, b_ * 128:(b_ + 1) * 128],
                                                      y1b[:, blk * 128:(blk + 1) * 128], ident[:])
                                env.inc("eT", ins, 1)
                                w("vector", "eT")
                                ev["y1T"][(i, grp)] = env.inc("eV", V.tensor_copy(
                                    y1T_t[:, grp * 8:(grp + 1) * 8, :],
                                    ptb[:].rearrange("p (c s) -> p c s", s=128)), 1)

                            w("tensor", "eV", ev["y1T"][(i, 3)])
                            for half in range(2):
                                for kk in range(32):
                                    T.matmul(pE[2 + half][:], y1T_t[:, kk, :],
                                             W2_sb[:, kk, half * 512:(half + 1) * 512],
                                             start=(kk == 0), stop=False)
                                ins = T.matmul(pE[2 + half][:], onesb[:],
                                               b2rowb[0:1, half * 512:(half + 1) * 512],
                                               start=False, stop=True)
                            ev["w2"][i] = env.inc("eT", ins, 1)
                            w("vector", "eT", ev["w2"][i])
                            w("vector", "dW", ev["s2_ld"][i])
                            w("vector", "dS")
                            V.tensor_tensor(outt[:, 0:512], s2t2[:, 0:512], pE[2][:], ALU.add)
                            ev["out"][i] = env.inc("eV", V.tensor_tensor(
                                outt[:, 512:1024], s2t2[:, 512:1024], pE[3][:], ALU.add), 1)
                            w("scalar", "eV", ev["out"][i])
                            env.inc("dS", SC.dma_start(outv[i], outt[:]), 16)

                        w("scalar", "dS")
                        w("sync", "dX")
                        w("sync", "dW")

                    run_phase(nc, allsems["D2"], D2)

        return nc


def _host_inputs(inputs, S=4096):
    f32 = np.float32
    bf = ml_dtypes.bfloat16
    x = np.asarray(inputs["L0_attn_x"], dtype=f32)

    def col8(a):
        return np.ascontiguousarray(np.asarray(a, dtype=f32).reshape(8, 128).T)

    base = {
        "w_l0": np.ascontiguousarray(np.asarray(inputs["L0_attn_W"], dtype=f32)),
        "k8": np.ascontiguousarray(np.asarray(inputs["L0_ffn_kernel"]).astype(bf)),
        "scales8": np.array([[np.asarray(inputs["L0_ffn_input_scale"], dtype=f32).reshape(()),
                              np.asarray(inputs["L0_ffn_kernel_scale"], dtype=f32).reshape(())]], dtype=f32),
        "bias8": np.asarray(inputs["L0_ffn_bias"]).astype(bf).reshape(1, D),
        "wq": np.ascontiguousarray(np.asarray(inputs["L1_attn_Wq"], dtype=f32).reshape(D, D)),
        "wk": np.ascontiguousarray(np.asarray(inputs["L1_attn_Wk"], dtype=f32).reshape(D, D)),
        "wv": np.ascontiguousarray(np.asarray(inputs["L1_attn_Wv"], dtype=f32).reshape(D, D)),
        "bk": np.asarray(inputs["L1_attn_bk"], dtype=f32).reshape(1, D),
        "bv": np.asarray(inputs["L1_attn_bv"], dtype=f32).reshape(1, D),
        "wo": np.ascontiguousarray(np.asarray(inputs["L1_attn_Wo"], dtype=f32).reshape(D, D)),
        "bo": np.asarray(inputs["L1_attn_bo"], dtype=f32).reshape(1, D),
        "w1": np.ascontiguousarray(np.asarray(inputs["L1_attn_W1"], dtype=f32)),
        "b1": np.asarray(inputs["L1_attn_b1"], dtype=f32).reshape(1, 4096),
        "w2": np.ascontiguousarray(np.asarray(inputs["L1_attn_W2"], dtype=f32)),
        "b2": np.asarray(inputs["L1_attn_b2"], dtype=f32).reshape(1, D),
        "wg": np.ascontiguousarray(np.asarray(inputs["L1_ffn_w_gate"], dtype=f32)),
        "wu": np.ascontiguousarray(np.asarray(inputs["L1_ffn_w_up"], dtype=f32)),
        "wd": np.ascontiguousarray(np.asarray(inputs["L1_ffn_w_down"], dtype=f32)),
        "ident": np.eye(128, dtype=bf),
        "mask": np.ascontiguousarray(np.triu(np.full((128, 128), NEG, dtype=f32), 1)),
        "bq_col": col8(inputs["L1_attn_bq"]),
        "g1_col": col8(inputs["L1_attn_ln1_scale"]),
        "b1l_col": col8(inputs["L1_attn_ln1_bias"]),
        "g2_col": col8(inputs["L1_attn_ln2_scale"]),
        "b2l_col": col8(inputs["L1_attn_ln2_bias"]),
    }
    in_maps = []
    for b in range(x.shape[0]):
        m = dict(base)
        m["x0"] = np.ascontiguousarray(x[b, :S])
        in_maps.append(m)
    return in_maps


def kernel(**inputs):
    nc = build_program(S=4096)
    in_maps = _host_inputs(inputs, S=4096)
    res = run_bass_kernel_spmd(nc, in_maps, core_ids=list(range(8)))
    return np.stack([np.asarray(r["out"], dtype=np.float32) for r in res.results])


# version 30
# speedup vs baseline: 1.0441x; 1.0441x over previous
import numpy as np
import ml_dtypes
from contextlib import ExitStack

import concourse.bass as bass
import concourse.mybir as mybir
from concourse.bass_utils import run_bass_kernel_spmd

F32 = mybir.dt.float32
BF16 = mybir.dt.bfloat16
F8 = mybir.dt.float8e4
AF = mybir.ActivationFunctionType
ALU = mybir.AluOpType
AX = mybir.AxisListType

D = 1024
NEG = -1.0e30


class _Null:
    def __getattr__(self, k):
        return lambda *a, **kw: None


_NULL = _Null()


class Env:
    def __init__(self, sems):
        self.sems = sems
        self.counts = {k: 0 for k in sems}

    def inc(self, name, instr, n=1):
        if instr is not None:
            instr.then_inc(self.sems[name], n)
        self.counts[name] += n
        return self.counts[name]

    def val(self, name):
        return self.counts[name]


SEM_NAMES = ["dX", "dX2", "dW", "dW2", "dS", "dS2", "dS3", "dG", "dG2", "eT", "eV", "eS", "eG"]


def run_phase(nc, sems, body):
    with nc.Block() as blk:
        blk.sync(lambda e: body(Env(sems), "sync", e))
        blk.gpsimd(lambda e: body(Env(sems), "gpsimd", e))
        blk.vector(lambda e: body(Env(sems), "vector", e))
        blk.scalar(lambda e: body(Env(sems), "scalar", e))
        blk.tensor(lambda e: body(Env(sems), "tensor", e))


def proxies(cur, eng):
    Y = lambda n: eng if cur == n else _NULL
    return Y("sync"), Y("gpsimd"), Y("vector"), Y("scalar"), Y("tensor")


def mk_waiter(env, cur, eng):
    def w(name, sem, v=None):
        if cur == name:
            eng.wait_ge(env.sems[sem], env.val(sem) if v is None else v)
    return w




def fp4_chain(env, V, w, src_ap, pw, B, pre_wait=None):
    nblk = pw // 16
    if pre_wait is not None:
        pre_wait()
    V.tensor_reduce(B["qamax"][:, 0:nblk],
                    src_ap.rearrange("p (b k) -> p b k", k=16),
                    AX.X, ALU.max, apply_absolute_value=True)
    V.drain()
    V.tensor_scalar(B["qscale"][:, 0:nblk], B["qamax"][:, 0:nblk],
                    0.16666666666666666, 1e-12, ALU.mult, ALU.max)
    V.drain()
    V.reciprocal(B["qrecip"][:, 0:nblk], B["qscale"][:, 0:nblk])
    V.drain()
    V.tensor_copy(B["qrecipb"][:, 0:nblk], B["qrecip"][:, 0:nblk])
    V.drain()
    V.tensor_scalar(B["qscaleb"][:, 0:nblk], B["qscale"][:, 0:nblk], 0.5, None, ALU.mult)
    V.drain()
    V.tensor_tensor(B["qxn"][:, 0:pw].rearrange("p (b k) -> p b k", k=16),
                    src_ap.rearrange("p (b k) -> p b k", k=16),
                    B["qrecipb"][:, 0:nblk].unsqueeze(-1).broadcast_to((128, nblk, 16)),
                    ALU.mult)
    V.drain()
    V.tensor_scalar(B["qtmp"][:, 0:pw], B["qxn"][:, 0:pw], 0.0, -2.0, ALU.is_lt, ALU.mult)
    V.drain()
    V.tensor_scalar(B["qtmp"][:, 0:pw], B["qtmp"][:, 0:pw], 1.0, None, ALU.add)
    V.drain()
    V.tensor_tensor(B["qt"][:, 0:pw], B["qxn"][:, 0:pw], B["qtmp"][:, 0:pw], ALU.mult)
    V.drain()
    V.tensor_scalar(B["qacc"][:, 0:pw], B["qt"][:, 0:pw], 5.0, 2.0, ALU.is_gt, ALU.mult)
    V.drain()
    V.scalar_tensor_tensor(B["qacc"][:, 0:pw], B["qt"][:, 0:pw], 2.5, B["qacc"][:, 0:pw], ALU.is_gt, ALU.add)
    V.drain()
    V.scalar_tensor_tensor(B["qacc"][:, 0:pw], B["qt"][:, 0:pw], 3.5, B["qacc"][:, 0:pw], ALU.is_gt, ALU.add)
    V.drain()
    V.tensor_scalar(B["qacc"][:, 0:pw], B["qacc"][:, 0:pw], 2.0, None, ALU.mult)
    V.drain()
    for mj in (0.25, 0.75, 1.25, 1.75):
        V.scalar_tensor_tensor(B["qacc"][:, 0:pw], B["qt"][:, 0:pw], mj, B["qacc"][:, 0:pw], ALU.is_gt, ALU.add)
        V.drain()
    V.tensor_tensor(B["qtmp"][:, 0:pw].rearrange("p (b k) -> p b k", k=16),
                    B["qtmp"][:, 0:pw].rearrange("p (b k) -> p b k", k=16),
                    B["qscaleb"][:, 0:nblk].unsqueeze(-1).broadcast_to((128, nblk, 16)),
                    ALU.mult)
    V.drain()
    return env.inc("eV", V.tensor_tensor(B["qxn"][:, 0:pw], B["qacc"][:, 0:pw],
                                         B["qtmp"][:, 0:pw], ALU.mult), 1)


def emit_ln(env, V, SC, w, x_ap, out_ap, stats, mv, rstd, nmr, eps, out_dtype_note=None):
    w("vector", "eS")
    V.bn_stats(stats[:, 0:6], x_ap[:, 0:512])
    V.bn_stats(stats[:, 6:12], x_ap[:, 512:1024])
    V.drain()
    V.bn_aggr(mv[:], stats[:])
    V.drain()
    env.inc("eV", V.tensor_scalar(rstd[:], mv[:, 1:2], eps, None, ALU.add), 1)
    w("scalar", "eV")
    env.inc("eS", SC.sqrt(rstd[:], rstd[:]), 1)
    w("vector", "eS")
    V.reciprocal(rstd[:], rstd[:])
    V.drain()
    env.inc("eV", V.tensor_scalar(nmr[:], mv[:, 0:1], rstd[:, 0:1], -1.0,
                                  ALU.mult, ALU.mult), 1)
    w("scalar", "eV")
    return env.inc("eS", SC.activation(out_ap, x_ap, AF.Identity,
                                       bias=nmr[:, 0:1], scale=rstd[:, 0:1]), 1)


def build_program(S=4096, stop_after=None):
    NT = S // 128
    nc = bass.Bass("TRN2", target_bir_lowering=False)

    x0_d = nc.dram_tensor("x0", [S, D], F32, kind="ExternalInput")
    wl0_d = nc.dram_tensor("w_l0", [2 * D, D], F32, kind="ExternalInput")
    k8_d = nc.dram_tensor("k8", [D, D], BF16, kind="ExternalInput")
    sc8_d = nc.dram_tensor("scales8", [1, 2], F32, kind="ExternalInput")
    b8_d = nc.dram_tensor("bias8", [1, D], BF16, kind="ExternalInput")
    wq_d = nc.dram_tensor("wq", [D, D], F32, kind="ExternalInput")
    wk_d = nc.dram_tensor("wk", [D, D], F32, kind="ExternalInput")
    wv_d = nc.dram_tensor("wv", [D, D], F32, kind="ExternalInput")
    bk_d = nc.dram_tensor("bk", [1, D], F32, kind="ExternalInput")
    bv_d = nc.dram_tensor("bv", [1, D], F32, kind="ExternalInput")
    wo_d = nc.dram_tensor("wo", [D, D], F32, kind="ExternalInput")
    bo_d = nc.dram_tensor("bo", [1, D], F32, kind="ExternalInput")
    w1_d = nc.dram_tensor("w1", [D, 4096], F32, kind="ExternalInput")
    b1_d = nc.dram_tensor("b1", [1, 4096], F32, kind="ExternalInput")
    w2_d = nc.dram_tensor("w2", [4096, D], F32, kind="ExternalInput")
    b2_d = nc.dram_tensor("b2", [1, D], F32, kind="ExternalInput")
    wg_d = nc.dram_tensor("wg", [2816, D], F32, kind="ExternalInput")
    wu_d = nc.dram_tensor("wu", [2816, D], F32, kind="ExternalInput")
    wd_d = nc.dram_tensor("wd", [D, 2816], F32, kind="ExternalInput")
    id_d = nc.dram_tensor("ident", [128, 128], BF16, kind="ExternalInput")
    mask_d = nc.dram_tensor("mask", [128, 128], F32, kind="ExternalInput")
    bq_c_d = nc.dram_tensor("bq_col", [128, 8], F32, kind="ExternalInput")
    g1_c_d = nc.dram_tensor("g1_col", [128, 8], F32, kind="ExternalInput")
    b1l_c_d = nc.dram_tensor("b1l_col", [128, 8], F32, kind="ExternalInput")
    g2_c_d = nc.dram_tensor("g2_col", [128, 8], F32, kind="ExternalInput")
    b2l_c_d = nc.dram_tensor("b2l_col", [128, 8], F32, kind="ExternalInput")

    out_d = nc.dram_tensor("out", [S, D], F32, kind="ExternalOutput")

    scr_x1 = nc.dram_tensor("scr_x1", [S, D], F32)
    scr_xn1 = nc.dram_tensor("scr_xn1", [S, D], F32)
    scr_xq4T = nc.dram_tensor("scr_xq4T", [8, 128, S], BF16)
    scr_phiqT = nc.dram_tensor("scr_phiqT", [64, 16, S], BF16)
    scr_p1 = nc.dram_tensor("scr_p1", [S, D], F32)
    scr_wgq = nc.dram_tensor("scr_wgq", [8, 128, 2816], BF16)
    scr_wuq = nc.dram_tensor("scr_wuq", [8, 128, 2816], BF16)
    scr_wdq = nc.dram_tensor("scr_wdq", [22, 128, D], BF16)
    scr_y0T = nc.dram_tensor("scr_y0T", [8, 128, S], BF16)
    scr_s2 = nc.dram_tensor("scr_s2", [S, D], F32)

    es = ExitStack()
    with es:
        SBo = lambda n, sh, dt: es.enter_context(nc.sbuf_tensor(n, sh, dt))
        ident = SBo("identS", [128, 128], BF16)
        onesb = SBo("onesb", [1, 128], BF16)
        ones_s = SBo("ones_s", [128, 1], BF16)
        c001 = SBo("c001", [128, 1], F32)
        allsems = {}
        for ph in ("P0", "AB", "C1", "C2", "D1", "D2", "TL"):
            allsems[ph] = {n: es.enter_context(nc.semaphore(ph + n)) for n in SEM_NAMES}

        l0es = ExitStack()
        with l0es:
            SL = lambda n, sh, dt: l0es.enter_context(nc.sbuf_tensor(n, sh, dt))
            maskt = SL("maskS", [128, 128], F32)
            WT = SL("WT", [128, 8, 2 * D], BF16)
            K8Q = SL("K8Q", [128, 8, D], F8)
            b8t = SL("b8t", [128, D], BF16)
            rs_in = SL("rs_in", [128, 1], F32)
            ss8 = SL("ss8", [128, 1], F32)

            p0es = ExitStack()
            with p0es:
                SP = lambda n, sh, dt: p0es.enter_context(nc.sbuf_tensor(n, sh, dt))
                w_oc = SP("w_oc", [128, 16, D], BF16)
                k8_sb = SP("k8_sb", [128, 8, D], BF16)
                sc_sb = SP("sc_sb", [1, 2], F32)
                sc_tmp = SP("sc_tmp", [1, 4], F32)
                rs_k = SP("rs_k", [128, 1], F32)
                b8row = SP("b8row", [1, D], BF16)
                onesf = SP("onesf", [1, 128], F32)
                PA = p0es.enter_context(nc.psum_tensor("p0A", [128, 1024], BF16))
                PB = p0es.enter_context(nc.psum_tensor("p0B", [128, 1024], BF16))
                PC = p0es.enter_context(nc.psum_tensor("p0C", [128, 512], F32))
                PD = p0es.enter_context(nc.psum_tensor("p0D", [128, 512], F32))

                def P0(env, cur, eng):
                    SY, GP, V, SC, T = proxies(cur, eng)
                    w = mk_waiter(env, cur, eng)

                    env.inc("dX", SY.dma_start(ident[:], id_d[:]), 16)
                    env.inc("dX", SY.dma_start(maskt[:], mask_d[:]), 16)
                    env.inc("dX", SY.dma_start(sc_sb[:], sc8_d[:]), 16)
                    env.inc("dX", SY.dma_start(b8row[:], b8_d[:]), 16)
                    k8v = k8_d.ap().rearrange("(c p) o -> c p o", p=128)
                    for c in range(8):
                        env.inc("dW", SY.dma_start(k8_sb[:, c, :], k8v[c]), 16)
                    wv_ = wl0_d.ap().rearrange("(t p) c -> t p c", p=128)
                    for t_ in range(16):
                        env.inc("dG", GP.dma_start(w_oc[:, t_, :], wv_[t_]), 16)

                    w("vector", "dX")
                    V.reciprocal(sc_tmp[:, 0:1], sc_sb[:, 0:1])
                    V.reciprocal(sc_tmp[:, 2:3], sc_sb[:, 1:2])
                    env.inc("eV", V.tensor_tensor(sc_tmp[:, 1:2], sc_sb[:, 0:1],
                                                  sc_sb[:, 1:2], ALU.mult), 1)
                    GP.memset(onesb[:], 1.0)
                    GP.memset(c001[:], 0.001)
                    env.inc("eG", GP.memset(ones_s[:], 1.0), 1)
                    V.memset(onesf[:], 1.0)
                    env.inc("eV", V.drain(), 1)
                    w("tensor", "eV")
                    T.matmul(PC[:, 0:1], onesf[:], sc_tmp[:, 0:1], start=True, stop=True)
                    T.matmul(PC[:, 1:2], onesf[:], sc_tmp[:, 1:2], start=True, stop=True)
                    env.inc("eT", T.matmul(PC[:, 2:3], onesf[:], sc_tmp[:, 2:3],
                                           start=True, stop=True), 1)
                    w("vector", "eT")
                    V.tensor_copy(rs_in[:], PC[:, 0:1])
                    V.tensor_copy(ss8[:], PC[:, 1:2])
                    env.inc("eV", V.tensor_copy(rs_k[:], PC[:, 2:3]), 1)
                    V.drain()

                    w("vector", "dW")
                    env.inc("eV", V.tensor_scalar(
                        K8Q[:].rearrange("p c o -> p (c o)"),
                        k8_sb[:].rearrange("p c o -> p (c o)"),
                        rs_k[:, 0:1], None, ALU.mult), 1)

                    w("tensor", "eG")
                    w("tensor", "eV")
                    T.matmul(PC[:], onesb[:], b8row[:, 0:512], start=True, stop=True)
                    env.inc("eT", T.matmul(PD[:], onesb[:], b8row[:, 512:1024],
                                           start=True, stop=True), 1)
                    w("vector", "eT")
                    V.tensor_copy(b8t[:, 0:512], PC[:])
                    env.inc("eV", V.tensor_copy(b8t[:, 512:1024], PD[:]), 1)

                    w("tensor", "dG")
                    evst = env.val("eV")
                    for t_ in range(16):
                        pt = PB if (t_ % 2) else PA
                        if t_ >= 2:
                            w("tensor", "eV", evst + (t_ - 1))
                        ins = None
                        for cb in range(8):
                            ins = T.transpose(pt[:, cb * 128:(cb + 1) * 128],
                                              w_oc[:, t_, cb * 128:(cb + 1) * 128],
                                              ident[:])
                        env.inc("eT", ins, 1)
                        w("vector", "eT")
                        env.inc("eV", V.tensor_copy(
                            WT[:, :, t_ * 128:(t_ + 1) * 128],
                            pt[:].rearrange("p (c o) -> p c o", o=128)), 1)

                run_phase(nc, allsems["P0"], P0)

            abes = ExitStack()
            with abes:
                SA = lambda n, sh, dt: abes.enter_context(nc.sbuf_tensor(n, sh, dt))
                K_fm = SA("K_fm", [128, 8, S], BF16)
                V_tm = SA("V_tm", [128, NT, D], BF16)
                x0t = SA("x0t", [128, D], F32)
                xn32 = SA("xn32", [128, D], F32)
                xnbf = SA("xnbf", [128, D], BF16)
                xnT = SA("xnT", [128, 8, 128], BF16)
                xq8T = SA("xq8T", [128, 8, 128], F8)
                ybf = SA("ybf", [128, D], BF16)
                stats = SA("stats", [128, 12], F32)
                mv = SA("mv", [128, 2], F32)
                rstd = SA("rstd", [128, 1], F32)
                nmr = SA("nmr", [128, 1], F32)
                negm = SA("negm", [128, 1], F32)
                denp = SA("denp", [128, 8], F32)
                den = SA("den", [128, 1], F32)
                rden = SA("rden", [128, 1], F32)
                sco = SA("sco", [128, S], BF16)
                Pb = [SA("Pb0", [128, 512], BF16), SA("Pb1", [128, 512], BF16)]
                PTs = [SA("PTs0", [128, 512], BF16), SA("PTs1", [128, 512], BF16)]
                wraw = SA("wraw", [128, 512], F32)
                qxnp = [SA("qxn0", [128, 512], BF16), SA("qxn1", [128, 512], BF16)]
                qt = SA("qt", [128, 512], BF16)
                qacc = SA("qacc", [128, 512], BF16)
                qtmp = SA("qtmp", [128, 512], BF16)
                qstage = SA("qstage", [128, 512], BF16)
                qamax = SA("qamax", [128, 32], F32)
                qscale = SA("qscale", [128, 32], F32)
                qrecip = SA("qrecip", [128, 32], F32)
                qrecipb = SA("qrecipb", [128, 32], BF16)
                qscaleb = SA("qscaleb", [128, 32], BF16)
                ABQB = {"qamax": qamax, "qscale": qscale, "qrecip": qrecip,
                        "qrecipb": qrecipb, "qscaleb": qscaleb,
                        "qxn": None, "qt": qt, "qacc": qacc, "qtmp": qtmp}

                pKV = abes.enter_context(nc.psum_tensor("pKV", [128, 512], F32))
                pKV2 = abes.enter_context(nc.psum_tensor("pKV2", [128, 512], F32))
                pT1 = abes.enter_context(nc.psum_tensor("pT1", [128, 1024], BF16))
                pT2 = abes.enter_context(nc.psum_tensor("pT2", [128, 1024], BF16))
                pF1 = abes.enter_context(nc.psum_tensor("pF1", [128, 512], F32))
                pF2 = abes.enter_context(nc.psum_tensor("pF2", [128, 512], F32))
                pS1 = abes.enter_context(nc.psum_tensor("pS1", [128, 512], F32))
                pS2 = abes.enter_context(nc.psum_tensor("pS2", [128, 512], F32))

                wq_jobs = [(wg_d, scr_wgq, 22, 2), (wu_d, scr_wuq, 22, 2),
                           (wd_d, scr_wdq, 8, 6)]

                def AB(env, cur, eng):
                    SY, GP, V, SC, T = proxies(cur, eng)
                    w = mk_waiter(env, cur, eng)
                    x0v = x0_d.ap().rearrange("(t p) c -> t p c", p=128)
                    x1v = scr_x1.ap().rearrange("(t p) c -> t p c", p=128)

                    wq_state = {"job": 0, "tile": 0, "piece": 0, "chain_ev": 0,
                                "pending": None, "tpose_ev": {}, "stage_ev": 0,
                                "par": 0}

                    def wq_chain_piece():
                        ji = wq_state["job"]
                        if ji >= len(wq_jobs):
                            return None
                        dsrc, dscr, ntile, npiece = wq_jobs[ji]
                        t_, p_ = wq_state["tile"], wq_state["piece"]
                        if p_ + 1 < npiece:
                            wq_state["piece"] += 1
                        else:
                            wq_state["piece"] = 0
                            if t_ + 1 < ntile:
                                wq_state["tile"] += 1
                            else:
                                wq_state["tile"] = 0
                                wq_state["job"] += 1
                        ncols = [D, D, 2816][ji]
                        pw = min(512, ncols - p_ * 512)
                        par = wq_state["par"]
                        wq_state["par"] = 1 - par
                        src_ = dsrc.ap()[t_ * 128:(t_ + 1) * 128, p_ * 512:p_ * 512 + pw]
                        w("sync", "eV", wq_state["chain_ev"])
                        env.inc("dW", SY.dma_start(wraw[:, 0:pw], src_), 16)
                        w("vector", "dW")
                        QB_ = dict(ABQB)
                        QB_["qxn"] = qxnp[par]
                        wq_state["chain_ev"] = fp4_chain(
                            env, V, w, wraw[:, 0:pw], pw, QB_,
                            pre_wait=lambda: w("vector", "eT",
                                               wq_state["tpose_ev"].get(par, 0)))
                        return (dscr, t_, p_, pw, par, wq_state["chain_ev"])

                    def wq_tpose_piece(info):
                        dscr, t_, p_, pw, par, chev = info
                        nb = pw // 128
                        w("tensor", "eV", chev)
                        w("tensor", "eS", wq_state.get("stage_last", 0))
                        ins = None
                        for b_ in range(nb):
                            ins = T.transpose(pT2[:, b_ * 128:(b_ + 1) * 128],
                                              qxnp[par][:, b_ * 128:(b_ + 1) * 128], ident[:])
                        wq_state["tpose_ev"][par] = env.inc("eT", ins, 1)
                        w("scalar", "eT", wq_state["tpose_ev"][par])
                        w("scalar", "dS2")
                        wq_state["stage_last"] = env.inc("eS", SC.copy(qstage[:, 0:pw], pT2[:, 0:pw]), 1)
                        SC.drain()
                        for b_ in range(nb):
                            cb = p_ * 4 + b_
                            env.inc("dS2", SC.dma_start(
                                dscr.ap()[cb, :, t_ * 128:(t_ + 1) * 128],
                                qstage[:, b_ * 128:(b_ + 1) * 128]), 16)

                    def wq_step():
                        q = wq_state.setdefault("queue", [])
                        if len(q) >= 1:
                            wq_tpose_piece(q.pop(0))
                        nxt = wq_chain_piece()
                        if nxt is not None:
                            q.append(nxt)
                        return nxt is not None or bool(q)

                    evA = {"s01": {}, "nmr": {}, "apply": {}, "xnT": {}, "x0_ld": {},
                           "store": {}}

                    def emit_stats(i):
                        w("vector", "dX", evA["x0_ld"][i])
                        w("vector", "eS", evA["apply"].get(i - 1, 0))
                        V.bn_stats(stats[:, 0:6], x0t[:, 0:512])
                        V.bn_stats(stats[:, 6:12], x0t[:, 512:1024])
                        V.drain()
                        V.bn_aggr(mv[:], stats[:])
                        V.drain()
                        env.inc("eV", V.tensor_scalar(rstd[:], mv[:, 1:2], 1e-5, None, ALU.add), 1)
                        w("scalar", "eV")
                        w("scalar", "dX", evA["x0_ld"][i])
                        env.inc("eS", SC.sqrt(rstd[:], rstd[:]), 1)
                        w("vector", "eS")
                        V.reciprocal(rstd[:], rstd[:])
                        V.drain()
                        evA["nmr"][i] = env.inc("eV", V.tensor_scalar(
                            nmr[:], mv[:, 0:1], rstd[:, 0:1], -1.0, ALU.mult, ALU.mult), 1)

                    prev_sco = {"eS": 0, "eV": 0}
                    env.inc("dX", SY.dma_start(x0t[:], x0v[0]), 16)
                    evA["x0_ld"][0] = env.val("dX")
                    emit_stats(0)

                    for i in range(NT):
                        w("scalar", "eV", evA["nmr"][i])
                        w("scalar", "dS", evA["store"].get(i - 1, 0))
                        evA["apply"][i] = env.inc("eS", SC.activation(
                            xn32[:], x0t[:], AF.Identity,
                            bias=nmr[:, 0:1], scale=rstd[:, 0:1]), 1)
                        SC.drain()
                        w("scalar", "eT", evA["xnT"].get(i - 1, 0))
                        ev_xnbf = env.inc("eS", SC.copy(xnbf[:], xn32[:]), 1)

                        w("tensor", "eS", ev_xnbf)
                        w("tensor", "eS")
                        ins = None
                        for cb in range(8):
                            ins = T.transpose(pT2[:, cb * 128:(cb + 1) * 128],
                                              xnbf[:, cb * 128:(cb + 1) * 128], ident[:])
                        evA["xnT"][i] = env.inc("eT", ins, 1)
                        w("vector", "eT", evA["xnT"][i])
                        env.inc("eV", V.tensor_copy(
                            xnT[:], pT2[:].rearrange("p (c s) -> p c s", s=128)), 1)
                        V.drain()
                        env.inc("eV", V.tensor_scalar(
                            xq8T[:].rearrange("p c s -> p (c s)"),
                            xnT[:].rearrange("p c s -> p (c s)"),
                            rs_in[:, 0:1], None, ALU.mult), 1)
                        ev_xq8 = env.val("eV")

                        kv_evac = []
                        for oc in range(8):
                            pk = pKV if (oc % 2 == 0) else pKV2
                            w("tensor", "eV", ev_xq8)
                            if oc >= 2:
                                w("tensor", "eS", kv_evac[oc - 2])
                            ins = None
                            for cc in range(8):
                                ins = T.matmul(pk[:, 0:128],
                                               WT[:, cc, oc * 128:(oc + 1) * 128],
                                               xnT[:, cc, :],
                                               start=(cc == 0), stop=(cc == 7))
                            env.inc("eT", ins, 1)
                            w("scalar", "eT")
                            kv_evac.append(env.inc("eS", SC.copy(
                                K_fm[:, oc, i * 128:(i + 1) * 128], pk[:, 0:128]), 1))
                        for half in range(2):
                            pk = pKV if (half == 0) else pKV2
                            w("tensor", "eS", kv_evac[6 + half])
                            ins = None
                            for cc in range(8):
                                ins = T.matmul(pk[:],
                                               xnT[:, cc, :],
                                               WT[:, cc, 1024 + half * 512:1024 + (half + 1) * 512],
                                               start=(cc == 0), stop=(cc == 7))
                            env.inc("eT", ins, 1)
                            w("scalar", "eT")
                            kv_evac.append(env.inc("eS", SC.copy(
                                V_tm[:, i, half * 512:(half + 1) * 512], pk[:]), 1))
                        ev_kvdone = kv_evac[-1]

                        wq_step()

                        ins = None
                        for cc in range(8):
                            T.matmul(pF1[:], xq8T[:, cc, :], K8Q[:, cc, 0:512],
                                     start=(cc == 0), stop=(cc == 7))
                            ins = T.matmul(pF2[:], xq8T[:, cc, :], K8Q[:, cc, 512:1024],
                                           start=(cc == 0), stop=(cc == 7))
                        env.inc("eT", ins, 1)
                        w("scalar", "eT")
                        SC.activation(ybf[:, 0:512], pF1[:], AF.Identity, scale=ss8[:, 0:1])
                        env.inc("eS", SC.activation(ybf[:, 512:1024], pF2[:], AF.Identity,
                                                    scale=ss8[:, 0:1]), 1)
                        ev_f8evac = env.val("eS")
                        w("vector", "eS", ev_f8evac)
                        env.inc("eV", V.tensor_tensor(ybf[:], ybf[:], b8t[:], ALU.add), 1)
                        V.drain()
                        V.tensor_tensor(xn32[:], xn32[:], x0t[:], ALU.add)
                        V.drain()
                        evA["s01"][i] = env.inc("eV", V.tensor_tensor(
                            xn32[:], xn32[:], ybf[:], ALU.add), 1)

                        if i + 1 < NT:
                            w("sync", "eV", evA["s01"][i])
                            env.inc("dX", SY.dma_start(x0t[:], x0v[i + 1]), 16)
                            evA["x0_ld"][i + 1] = env.val("dX")
                            emit_stats(i + 1)

                        wq_step()

                        nk = (i + 1) * 128
                        ngrp = (nk + 511) // 512
                        sc_evac = {}
                        pair_ev = dict(prev_sco)
                        for g0 in range(0, ngrp, 2):
                            has2 = (g0 + 1 < ngrp)
                            w0 = min(512, nk - g0 * 512)
                            w1 = min(512, nk - (g0 + 1) * 512) if has2 else 0
                            w("tensor", "eV", max(pair_ev["eV"], 1) if pair_ev["eV"] else None)
                            w("tensor", "eS", max(pair_ev["eS"],
                                                  ev_kvdone if g0 >= ngrp - 2 else 0))
                            ins = None
                            for cc in range(8):
                                ins = T.matmul(pS1[:, 0:w0], xnT[:, cc, :],
                                               K_fm[:, cc, g0 * 512:g0 * 512 + w0],
                                               start=(cc == 0), stop=(cc == 7))
                                if has2:
                                    ins = T.matmul(pS2[:, 0:w1], xnT[:, cc, :],
                                                   K_fm[:, cc, (g0 + 1) * 512:(g0 + 1) * 512 + w1],
                                                   start=(cc == 0), stop=(cc == 7))
                            ev_smm = env.inc("eT", ins, 1)
                            for (g_, w_, pb_) in (((g0, w0, pS1),) + (((g0 + 1, w1, pS2),) if has2 else ())):
                                if g_ == ngrp - 1:
                                    w("scalar", "eT", ev_smm)
                                    ev_cp = None
                                    if w_ > 128:
                                        ev_cp = env.inc("eS", SC.copy(
                                            sco[:, g_ * 512:g_ * 512 + w_ - 128],
                                            pb_[:, 0:w_ - 128]), 1)
                                        SC.drain()
                                    w("vector", "eT", ev_smm)
                                    if ev_cp is not None:
                                        w("vector", "eS", ev_cp)
                                    ins = V.scalar_tensor_tensor(
                                        sco[:, g_ * 512 + w_ - 128:g_ * 512 + w_],
                                        pb_[:, w_ - 128:w_], 1.0, maskt[:],
                                        ALU.mult, ALU.add)
                                    pair_ev["eV"] = env.inc("eV", ins, 1)
                                else:
                                    w("scalar", "eT", ev_smm)
                                    ins = SC.copy(sco[:, g_ * 512:g_ * 512 + w_],
                                                  pb_[:, 0:w_])
                                    SC.drain()
                                    pair_ev["eS"] = env.inc("eS", ins, 1)
                        prev_sco = dict(pair_ev)
                        w("vector", "eS")
                        V.drain()
                        env.inc("eV", V.reduce_max(negm[:], sco[:, 0:nk], axis=AX.X,
                                                   negate=True), 1)
                        ev_negm = env.val("eV")

                        wq_step()

                        tpose_done = {}
                        pv_done = {}
                        tpose_evac = {}
                        for g in range(ngrp):
                            wg_ = min(512, nk - g * 512)
                            par = g % 2
                            ptb = pT1 if par == 0 else pT2
                            w("scalar", "eV", ev_negm)
                            if g >= 2:
                                w("scalar", "eT", tpose_done[g - 2])
                            env.inc("eS", SC.activation(Pb[par][:, 0:wg_],
                                                        sco[:, g * 512:g * 512 + wg_],
                                                        AF.Exp, bias=negm[:, 0:1], scale=1.0,
                                                        accum_out=denp[:, g:g + 1]), 1)
                            nkb = (wg_ + 127) // 128
                            w("tensor", "eS")
                            if g >= 2:
                                w("tensor", "eS", tpose_evac[g - 2])
                            ins = None
                            for kb_ in range(nkb):
                                ins = T.transpose(ptb[:, kb_ * 128:(kb_ + 1) * 128],
                                                  Pb[par][:, kb_ * 128:(kb_ + 1) * 128],
                                                  ident[:])
                            tpose_done[g] = env.inc("eT", ins, 1)
                            w("scalar", "eT", tpose_done[g])
                            if g >= 2:
                                w("scalar", "eT", pv_done[g - 2])
                            tpose_evac[g] = env.inc("eS", SC.copy(
                                PTs[par][:, 0:nkb * 128], ptb[:, 0:nkb * 128]), 1)
                            SC.drain()
                            w("tensor", "eS", tpose_evac[g])
                            ins = None
                            for kb_ in range(nkb):
                                kb = g * 4 + kb_
                                T.matmul(pF1[:], PTs[par][:, kb_ * 128:(kb_ + 1) * 128],
                                         V_tm[:, kb, 0:512],
                                         start=(kb == 0), stop=(kb == i))
                                ins = T.matmul(pF2[:], PTs[par][:, kb_ * 128:(kb_ + 1) * 128],
                                               V_tm[:, kb, 512:1024],
                                               start=(kb == 0), stop=(kb == i))
                            pv_done[g] = env.inc("eT", ins, 1)
                        w("vector", "eS")
                        V.reduce_sum(den[:], denp[:, 0:ngrp], axis=AX.X)
                        V.drain()
                        V.reciprocal(rden[:], den[:])
                        V.drain()
                        w("vector", "eT")
                        V.scalar_tensor_tensor(xn32[:, 0:512], pF1[:], rden[:, 0:1],
                                               xn32[:, 0:512], ALU.mult, ALU.add)
                        V.drain()
                        env.inc("eV", V.scalar_tensor_tensor(xn32[:, 512:1024], pF2[:],
                                                             rden[:, 0:1], xn32[:, 512:1024],
                                                             ALU.mult, ALU.add), 1)
                        w("scalar", "eV")
                        env.inc("dS", SC.dma_start(x1v[i], xn32[:]), 16)
                        evA["store"][i] = env.val("dS")

                        wq_step()

                    while wq_step():
                        pass
                    for info in wq_state.get("queue", []):
                        wq_tpose_piece(info)
                    wq_state["queue"] = []
                    w("scalar", "dS")
                    w("scalar", "dS2")
                    w("sync", "dX")
                    w("sync", "dW")
                    w("gpsimd", "dG")

                run_phase(nc, allsems["AB"], AB)

        if stop_after == "AB":
            def TAIL(env, cur, eng):
                SY, GP, V, SC, T = proxies(cur, eng)
                w = mk_waiter(env, cur, eng)
                env.inc("dX", SY.dma_start(out_d.ap(), scr_x1.ap()), 16)
                w("sync", "dX")
            run_phase(nc, allsems["TL"], TAIL)
            return nc

        l1es = ExitStack()
        with l1es:
            kv16 = l1es.enter_context(nc.sbuf_tensor("kv16", [64, 16 * 66], F32))
            kv16b = l1es.enter_context(nc.sbuf_tensor("kv16b", [64, 16 * 66], BF16))
            c1es = ExitStack()
            with c1es:
                SC1 = lambda n, sh, dt: c1es.enter_context(nc.sbuf_tensor(n, sh, dt))
                Wq_sb = SC1("Wq_sb", [128, 8, D], BF16)
                Wk_sb = SC1("Wk_sb", [128, 8, D], BF16)
                Wv_sb = SC1("Wv_sb", [128, 8, D], BF16)
                bqe = SC1("bqe", [128, 8], F32)
                g1col = SC1("g1col", [128, 8], F32)
                b1lb = SC1("b1lb", [128, 8], BF16)
                radj = SC1("radj", [128, 8], F32)
                bkrow = SC1("bkrow", [1, D], F32)
                bvrow = SC1("bvrow", [1, D], F32)
                bkrow_b = SC1("bkrow_b", [1, D], BF16)
                bvrow_b = SC1("bvrow_b", [1, D], BF16)
                x1t = SC1("x1t_c1", [128, 4, D], F32)
                xn1t = SC1("xn1t_c1", [128, 4, D], F32)
                n1b = SC1("n1b", [128, D], BF16)
                n1T = SC1("n1T", [128, 8, 512], BF16)
                phik_t = SC1("phik_t", [128, 4, D], BF16)
                v_t = SC1("v_t", [128, 4, D], BF16)
                phiq_st = [SC1("phiq_st0", [128, 512], BF16),
                           SC1("phiq_st1", [128, 512], BF16)]
                xq_st = SC1("xq_st", [128, D], BF16)
                stats = SC1("statsC", [128, 12], F32)
                mv = SC1("mvC", [128, 2], F32)
                rstd = SC1("rstdC", [128, 1], F32)
                nmr = SC1("nmrC", [128, 1], F32)
                QB = {
                    "qamax": SC1("c1qamax", [128, 64], F32),
                    "qscale": SC1("c1qscale", [128, 64], F32),
                    "qrecip": SC1("c1qrecip", [128, 64], F32),
                    "qrecipb": SC1("c1qrecipb", [128, 64], BF16),
                    "qscaleb": SC1("c1qscaleb", [128, 64], BF16),
                    "qxn": SC1("c1qxn", [128, D], BF16),
                    "qt": SC1("c1qt", [128, D], BF16),
                    "qacc": SC1("c1qacc", [128, D], BF16),
                    "qtmp": SC1("c1qtmp", [128, D], BF16),
                }
                pP = [c1es.enter_context(nc.psum_tensor("c1p%d" % b_, [128, 512], F32))
                      for b_ in range(6)]
                pTa = c1es.enter_context(nc.psum_tensor("c1ta", [128, 1024], BF16))
                pTb = c1es.enter_context(nc.psum_tensor("c1tb", [128, 1024], BF16))

                NG = S // 512

                def C1(env, cur, eng):
                    SY, GP, V, SC, T = proxies(cur, eng)
                    w = mk_waiter(env, cur, eng)
                    x1v = scr_x1.ap().rearrange("(t p) c -> t p c", p=128)
                    xn1v = scr_xn1.ap().rearrange("(t p) c -> t p c", p=128)

                    for (wd_, sb_) in ((wq_d, Wq_sb), (wk_d, Wk_sb), (wv_d, Wv_sb)):
                        wvv = wd_.ap().rearrange("(c p) o -> c p o", p=128)
                        for cc in range(8):
                            env.inc("dG", GP.dma_start(sb_[:, cc, :], wvv[cc]), 16)
                    env.inc("dW", SY.dma_start(bqe[:], bq_c_d.ap()), 16)
                    env.inc("dW", SY.dma_start(g1col[:], g1_c_d.ap()), 16)
                    env.inc("dW", SY.dma_start(radj[:], b1l_c_d.ap()), 16)
                    env.inc("dW", SY.dma_start(bkrow[:], bk_d.ap()), 16)
                    env.inc("dW", SY.dma_start(bvrow[:], bv_d.ap()), 16)
                    w("vector", "dW")
                    env.inc("eV", V.tensor_scalar(bqe[:], bqe[:], 0.001, None, ALU.add), 1)
                    V.drain()
                    env.inc("eV", V.tensor_copy(b1lb[:], radj[:]), 1)
                    V.memset(kv16[:], 0.0)
                    env.inc("eV", V.drain(), 1)
                    ev_pro = env.val("eV")

                    w("tensor", "dG")
                    w("tensor", "eV", ev_pro)
                    ins = None
                    for hc in range(8):
                        for cc in range(8):
                            ins = T.matmul(pP[hc % 4][:, hc // 4:hc // 4 + 1],
                                           Wq_sb[:, cc, hc * 128:(hc + 1) * 128],
                                           b1lb[:, cc:cc + 1],
                                           start=(cc == 0), stop=(cc == 7))
                    env.inc("eT", ins, 1)
                    w("vector", "eT")
                    for hc in range(8):
                        ins = V.tensor_copy(radj[:, hc:hc + 1], pP[hc % 4][:, hc // 4:hc // 4 + 1])
                    env.inc("eV", ins, 1)
                    V.drain()
                    env.inc("eV", V.tensor_tensor(bqe[:], bqe[:], radj[:], ALU.add), 1)
                    ev_bqe = env.val("eV")
                    w("tensor", "eV", ev_bqe)
                    ins = None
                    for bi, (sb_, half) in enumerate(((Wk_sb, 0), (Wk_sb, 1),
                                                      (Wv_sb, 0), (Wv_sb, 1))):
                        for cc in range(8):
                            ins = T.matmul(pP[bi][0:1, 0:512],
                                           b1lb[:, cc:cc + 1],
                                           sb_[:, cc, half * 512:(half + 1) * 512],
                                           start=(cc == 0), stop=(cc == 7))
                    env.inc("eT", ins, 1)
                    w("vector", "eT")
                    V.tensor_tensor(bkrow[0:1, 0:512], bkrow[0:1, 0:512], pP[0][0:1, 0:512], ALU.add)
                    V.tensor_tensor(bkrow[0:1, 512:1024], bkrow[0:1, 512:1024], pP[1][0:1, 0:512], ALU.add)
                    V.tensor_tensor(bvrow[0:1, 0:512], bvrow[0:1, 0:512], pP[2][0:1, 0:512], ALU.add)
                    env.inc("eV", V.tensor_tensor(bvrow[0:1, 512:1024], bvrow[0:1, 512:1024],
                                                  pP[3][0:1, 0:512], ALU.add), 1)
                    V.drain()
                    V.tensor_copy(bkrow_b[:], bkrow[:])
                    env.inc("eV", V.tensor_copy(bvrow_b[:], bvrow[:]), 1)
                    V.drain()
                    for sb_ in (Wq_sb, Wk_sb, Wv_sb):
                        for cc in range(8):
                            ins = V.tensor_scalar(sb_[:, cc, :], sb_[:, cc, :],
                                                  g1col[:, cc:cc + 1], None, ALU.mult)
                    env.inc("eV", ins, 1)
                    V.drain()
                    ev_wscaled = env.val("eV")

                    last_xq_tp = [0]
                    pq_store = {}
                    bankev = {}
                    evC = {"n1T": {}, "chain": {}, "qevac": {}, "kvevac": {}, "phv": {},
                           "kvadd": {}, "xqT": {}, "xqS": {}, "ln": {}}
                    for j in range(NG):
                        w("sync", "eS", evC["ln"].get(j - 1, 0))
                        for st in range(4):
                            env.inc("dX", SY.dma_start(x1t[:, st, :], x1v[4 * j + st]), 16)
                        ld_now = env.val("dX")

                        for st in range(4):
                            w("vector", "dX", ld_now)
                            w("scalar", "dX", ld_now)
                            emit_ln(env, V, SC, w, x1t[:, st, :], xn1t[:, st, :],
                                    stats, mv, rstd, nmr, 1e-5)
                            SC.drain()
                            env.inc("dS", SC.dma_start(xn1v[4 * j + st], xn1t[:, st, :]), 16)
                            w("vector", "eS")
                            emit_ln(env, V, SC, w, xn1t[:, st, :], n1b[:],
                                    stats, mv, rstd, nmr, 1e-6)
                            ev_n1 = env.val("eS")
                            evC["ln"][j] = ev_n1
                            w("tensor", "eS", ev_n1)
                            bname = "pTa" if st % 2 == 0 else "pTb"
                            w("tensor", "eS", bankev.get(bname, 0))
                            ptb = pTa if st % 2 == 0 else pTb
                            ins = None
                            for cb in range(8):
                                ins = T.transpose(ptb[:, cb * 128:(cb + 1) * 128],
                                                  n1b[:, cb * 128:(cb + 1) * 128], ident[:])
                            env.inc("eT", ins, 1)
                            w("scalar", "eT")
                            evC["n1T"][(j, st)] = env.inc("eS", SC.copy(
                                n1T[:, :, st * 128:(st + 1) * 128],
                                ptb[:].rearrange("p (c s) -> p c s", s=128)), 1)
                            bankev[bname] = evC["n1T"][(j, st)]
                        ev_n1T_all = evC["n1T"][(j, 3)]

                        for st in range(4):
                            ch = fp4_chain(env, V, w, xn1t[:, st, :], D, QB,
                                           pre_wait=lambda: w("vector", "eT", last_xq_tp[0]))
                            evC["chain"][(j, st)] = ch
                            w("tensor", "eV", ch)
                            bname = "pTb" if st % 2 == 0 else "pTa"
                            w("tensor", "eS", bankev.get(bname, 0))
                            ptq = pTb if st % 2 == 0 else pTa
                            ins = None
                            for cb in range(8):
                                ins = T.transpose(ptq[:, cb * 128:(cb + 1) * 128],
                                                  QB["qxn"][:, cb * 128:(cb + 1) * 128], ident[:])
                            last_xq_tp[0] = env.inc("eT", ins, 1)
                            w("scalar", "eT", last_xq_tp[0])
                            w("scalar", "dW2")
                            evC["xqS"][(j, st)] = env.inc("eS", SC.copy(xq_st[:], ptq[:]), 1)
                            bankev[bname] = evC["xqS"][(j, st)]
                            SC.drain()
                            for cb in range(8):
                                env.inc("dW2", SC.dma_start(
                                    scr_xq4T.ap()[cb, :, (4 * j + st) * 128:(4 * j + st + 1) * 128],
                                    xq_st[:, cb * 128:(cb + 1) * 128]), 16)

                        for hc in range(8):
                            pq = pP[hc % 2]
                            qpar = hc % 2
                            w("tensor", "eS", ev_n1T_all)
                            w("tensor", "eS", bankev.get("pP%d" % (hc % 2), 0))
                            if hc < 2:
                                w("tensor", "eV", evC["kvadd"].get((j - 1, 2), 0))
                            ins = None
                            for cc in range(8):
                                ins = T.matmul(pq[:], Wq_sb[:, cc, hc * 128:(hc + 1) * 128],
                                               n1T[:, cc, :], start=(cc == 0), stop=(cc == 7))
                            env.inc("eT", ins, 1)
                            qsem = "dS2" if qpar == 0 else "dS3"
                            w("scalar", "eT")
                            w("scalar", qsem, pq_store.get((j, hc - 2),
                                                           pq_store.get((j - 1, 6 + qpar), 0)))
                            SC.activation(phiq_st[qpar][:], pq[:], AF.Relu,
                                          bias=bqe[:, hc:hc + 1], scale=1.0)
                            SC.drain()
                            bankev["pP%d" % (hc % 2)] = env.inc("eS", SC.add(
                                phiq_st[qpar][:], phiq_st[qpar][:], c001[:, 0:1]), 1)
                            SC.drain()
                            env.inc(qsem, SC.dma_start(
                                scr_phiqT.ap()[:, 2 * hc, j * 512:(j + 1) * 512],
                                phiq_st[qpar][0:64, :]), 16)
                            pq_store[(j, hc)] = env.inc(qsem, SC.dma_start(
                                scr_phiqT.ap()[:, 2 * hc + 1, j * 512:(j + 1) * 512],
                                phiq_st[qpar][64:128, :]), 16)

                        for st in range(4):
                            for (sb_, brow, dst, isk) in ((Wk_sb, bkrow_b, phik_t, True),
                                                          (Wv_sb, bvrow_b, v_t, False)):
                                for half in range(2):
                                    pq = pP[2 + half]
                                    w("tensor", "eS", bankev.get("pP%d" % (2 + half), 0))
                                    if st == 0 and isk:
                                        w("tensor", "eV", evC["kvadd"].get((j - 1, 2), 0))
                                    for cc in range(8):
                                        T.matmul(pq[:], n1T[:, cc, st * 128:(st + 1) * 128],
                                                 sb_[:, cc, half * 512:(half + 1) * 512],
                                                 start=(cc == 0), stop=False)
                                    ins = T.matmul(pq[:], onesb[:],
                                                   brow[0:1, half * 512:(half + 1) * 512],
                                                   start=False, stop=True)
                                    env.inc("eT", ins, 1)
                                    w("scalar", "eT")
                                    if isk:
                                        SC.activation(dst[:, st, half * 512:(half + 1) * 512],
                                                      pq[:], AF.Relu, bias=c001[:, 0:1], scale=1.0)
                                        SC.drain()
                                        ins = SC.add(dst[:, st, half * 512:(half + 1) * 512],
                                                     dst[:, st, half * 512:(half + 1) * 512],
                                                     c001[:, 0:1])
                                    else:
                                        ins = SC.copy(dst[:, st, half * 512:(half + 1) * 512], pq[:])
                                    bankev["pP%d" % (2 + half)] = env.inc("eS", ins, 1)
                        ev_phiv = env.val("eS")

                        for gidx, grp in enumerate(((0, 6), (6, 12), (12, 16))):
                            nh = grp[1] - grp[0]
                            pb_ = pP[gidx]
                            w("tensor", "eS", ev_phiv)
                            w("tensor", "eV", evC["kvadd"].get((j, gidx - 1),
                                                               evC["kvadd"].get((j - 1, 2), 0)))
                            ins = None
                            for hi, h in enumerate(range(*grp)):
                                for st in range(4):
                                    T.matmul(pb_[0:64, hi * 66:hi * 66 + 64],
                                             phik_t[:, st, h * 64:(h + 1) * 64],
                                             v_t[:, st, h * 64:(h + 1) * 64],
                                             start=(hi == 0 and st == 0), stop=False)
                                    ins = T.matmul(pb_[0:64, hi * 66 + 64:hi * 66 + 65],
                                                   phik_t[:, st, h * 64:(h + 1) * 64],
                                                   ones_s[:],
                                                   start=False,
                                                   stop=(hi == nh - 1 and st == 3))
                            env.inc("eT", ins, 1)
                            w("vector", "eT")
                            ins = V.tensor_tensor(
                                kv16[0:64, grp[0] * 66:grp[0] * 66 + nh * 66],
                                kv16[0:64, grp[0] * 66:grp[0] * 66 + nh * 66],
                                pb_[0:64, 0:nh * 66], ALU.add)
                            evC["kvadd"][(j, gidx)] = env.inc("eV", ins, 1)
                            V.drain()

                    env.inc("eV", V.tensor_copy(kv16b[:], kv16[:]), 1)
                    w("scalar", "dS")
                    w("scalar", "dS2")
                    w("scalar", "dS3")
                    w("scalar", "dW2")
                    w("sync", "dX")
                    w("gpsimd", "dG")

                run_phase(nc, allsems["C1"], C1)

            if stop_after == "C1":
                def TAILC(env, cur, eng):
                    SY, GP, V, SC, T = proxies(cur, eng)
                    w = mk_waiter(env, cur, eng)
                    env.inc("dX", SY.dma_start(out_d.ap(), scr_xn1.ap()), 16)
                    w("sync", "dX")
                run_phase(nc, allsems["TL"], TAILC)
                return nc

            c2es = ExitStack()
            with c2es:
                SC2 = lambda n, sh, dt: c2es.enter_context(nc.sbuf_tensor(n, sh, dt))
                wgq_sb = SC2("wgq_sb", [128, 8, 2816], BF16)
                wuq_sb = SC2("wuq_sb", [128, 8, 2816], BF16)
                wdq_sb = SC2("wdq_sb", [128, 22, D], BF16)
                x1t2 = SC2("x1t2", [128, D], F32)
                xq4t = [SC2("xq4t0", [128, 8, 128], BF16), SC2("xq4t1", [128, 8, 128], BF16)]
                gs = SC2("gs", [128, 2816], BF16)
                us_ = SC2("us_", [128, 2816], BF16)
                hh = [SC2("hh0", [128, 2816], BF16), SC2("hh1", [128, 2816], BF16)]
                hqT = SC2("hqT", [128, 22, 128], BF16)
                p1t = SC2("p1t", [128, D], F32)
                QB2 = {
                    "qamax": SC2("c2qamax", [128, 176], F32),
                    "qscale": SC2("c2qscale", [128, 176], F32),
                    "qrecip": SC2("c2qrecip", [128, 176], F32),
                    "qrecipb": SC2("c2qrecipb", [128, 176], BF16),
                    "qscaleb": SC2("c2qscaleb", [128, 176], BF16),
                    "qxn": SC2("c2qxn", [128, 2816], BF16),
                    "qt": SC2("c2qt", [128, 2816], BF16),
                    "qacc": SC2("c2qacc", [128, 2816], BF16),
                    "qtmp": SC2("c2qtmp", [128, 2816], BF16),
                }
                pQ = [c2es.enter_context(nc.psum_tensor("c2p%d" % b_, [128, 512], F32))
                      for b_ in range(6)]
                pTa2 = c2es.enter_context(nc.psum_tensor("c2ta", [128, 1024], BF16))
                pTb2 = c2es.enter_context(nc.psum_tensor("c2tb", [128, 1024], BF16))

                OCH = [(0, 512), (512, 1024), (1024, 1536), (1536, 2048),
                       (2048, 2560), (2560, 2816)]

                def C2(env, cur, eng):
                    SY, GP, V, SC, T = proxies(cur, eng)
                    w = mk_waiter(env, cur, eng)
                    x1v = scr_x1.ap().rearrange("(t p) c -> t p c", p=128)
                    p1v = scr_p1.ap().rearrange("(t p) c -> t p c", p=128)

                    for cc in range(8):
                        env.inc("dW", SY.dma_start(wgq_sb[:, cc, :], scr_wgq.ap()[cc]), 16)
                    for cc in range(8):
                        env.inc("dW", SY.dma_start(wuq_sb[:, cc, :], scr_wuq.ap()[cc]), 16)
                    for kk in range(22):
                        env.inc("dW", SY.dma_start(wdq_sb[:, kk, :], scr_wdq.ap()[kk]), 16)
                    ld_w = env.val("dW")

                    NST = S // 128
                    ev = {"hmul": {}, "chain": {}, "hqT_T": {}, "hqT_S": {},
                          "down": {}, "p1": {}, "gu": {}, "silu": {}, "xq_ld": {},
                          "x1_ld": {}}

                    def stageA(st):
                        par = st % 2
                        xq = xq4t[par]
                        for p_ in range(6):
                            o0, o1 = OCH[p_]
                            bg = pQ[p_ % 2]
                            bu = pQ[2 + p_ % 2]
                            w("tensor", "dX", ev["xq_ld"][st])
                            w("tensor", "dW", ld_w)
                            prev = (st, p_ - 2) if p_ >= 2 else (st - 1, 4 + p_)
                            w("tensor", "eS", ev["silu"].get(prev, 0))
                            w("tensor", "eG", ev["hmul"].get(prev, 0))
                            ins = None
                            for cc in range(8):
                                T.matmul(bg[:, 0:o1 - o0], xq[:, cc, :],
                                         wgq_sb[:, cc, o0:o1],
                                         start=(cc == 0), stop=(cc == 7))
                                ins = T.matmul(bu[:, 0:o1 - o0], xq[:, cc, :],
                                               wuq_sb[:, cc, o0:o1],
                                               start=(cc == 0), stop=(cc == 7))
                            ev["gu"][(st, p_)] = env.inc("eT", ins, 1)
                            w("scalar", "eT", ev["gu"][(st, p_)])
                            w("scalar", "eG", ev["hmul"].get(prev, 0))
                            SC.activation(gs[:, o0:o1], bg[:, 0:o1 - o0], AF.Silu)
                            SC.drain()
                            ins = SC.copy(us_[:, o0:o1], bu[:, 0:o1 - o0])
                            SC.drain()
                            ev["silu"][(st, p_)] = env.inc("eS", ins, 1)
                            w("gpsimd", "eS", ev["silu"][(st, p_)])
                            hpar = hh[par]
                            ins = GP.tensor_tensor(hpar[:, o0:o1], gs[:, o0:o1],
                                                   us_[:, o0:o1], ALU.mult)
                            GP.drain()
                            ev["hmul"][(st, p_)] = env.inc("eG", ins, 1)
                        if st + 1 < NST:
                            w("sync", "eT", ev["gu"][(st, 5)])
                            for cb in range(8):
                                env.inc("dX", SY.dma_start(
                                    xq4t[(st + 1) % 2][:, cb, :],
                                    scr_xq4T.ap()[cb, :, (st + 1) * 128:(st + 2) * 128]), 16)
                            ev["xq_ld"][st + 1] = env.val("dX")

                    def stageB_V(st):
                        par = st % 2
                        def _pre():
                            w("vector", "eT", ev["hqT_T"].get(st - 1, 0))
                            w("vector", "eG", ev["hmul"][(st, 5)])
                        ch = fp4_chain(env, V, w, hh[par][:], 2816, QB2, pre_wait=_pre)
                        ev["chain"][st] = ch

                    def stageB(st):
                        ch = ev["chain"][st]
                        done = 0
                        gi = 0
                        prev_same_bank = {0: (st - 1, 2), 1: (st - 1, 1), 2: (st, 0)}
                        for grp in (8, 8, 6):
                            ptb = pTa2 if gi % 2 == 0 else pTb2
                            w("tensor", "eV", ch)
                            w("tensor", "eS", ev["hqT_S"].get(prev_same_bank[gi], 0))
                            ins = None
                            for b_ in range(grp):
                                ins = T.transpose(ptb[:, b_ * 128:(b_ + 1) * 128],
                                                  QB2["qxn"][:, (done + b_) * 128:(done + b_ + 1) * 128],
                                                  ident[:])
                            ev["hqT_T"][st] = env.inc("eT", ins, 1)
                            w("scalar", "eT", ev["hqT_T"][st])
                            ev["hqT_S"][(st, gi)] = env.inc("eS", SC.copy(
                                hqT[:, done:done + grp, :].rearrange("p k s -> p (k s)"),
                                ptb[:, 0:grp * 128]), 1)
                            done += grp
                            gi += 1
                        w("sync", "eV", ev["p1"].get(st - 1, 0))
                        env.inc("dW", SY.dma_start(x1t2[:], x1v[st]), 16)
                        ev["x1_ld"][st] = env.val("dW")
                        w("tensor", "eS", ev["hqT_S"][(st, 2)])
                        w("tensor", "eV", ev["p1"].get(st - 1, 0))
                        ins = None
                        for kk in range(22):
                            T.matmul(pQ[4][:], hqT[:, kk, :], wdq_sb[:, kk, 0:512],
                                     start=(kk == 0), stop=(kk == 21))
                            ins = T.matmul(pQ[5][:], hqT[:, kk, :], wdq_sb[:, kk, 512:1024],
                                           start=(kk == 0), stop=(kk == 21))
                        ev["down"][st] = env.inc("eT", ins, 1)
                        w("vector", "eT", ev["down"][st])
                        w("vector", "dW", ev["x1_ld"][st])
                        w("vector", "dS")
                        V.tensor_tensor(p1t[:, 0:512], x1t2[:, 0:512], pQ[4][:], ALU.add)
                        ev["p1"][st] = env.inc("eV", V.tensor_tensor(
                            p1t[:, 512:1024], x1t2[:, 512:1024], pQ[5][:], ALU.add), 1)
                        w("scalar", "eV", ev["p1"][st])
                        env.inc("dS", SC.dma_start(p1v[st], p1t[:]), 16)

                    for cb in range(8):
                        env.inc("dX", SY.dma_start(xq4t[0][:, cb, :],
                                                   scr_xq4T.ap()[cb, :, 0:128]), 16)
                    ev["xq_ld"][0] = env.val("dX")

                    for st in range(NST):
                        if st > 0:
                            stageB_V(st - 1)
                        stageA(st)
                        if st > 0:
                            stageB(st - 1)
                    stageB_V(NST - 1)
                    stageB(NST - 1)

                    w("scalar", "dS")
                    w("sync", "dX")
                    w("sync", "dW")

                run_phase(nc, allsems["C2"], C2)

            if stop_after == "C2":
                def TAILC2(env, cur, eng):
                    SY, GP, V, SC, T = proxies(cur, eng)
                    w = mk_waiter(env, cur, eng)
                    env.inc("dX", SY.dma_start(out_d.ap(), scr_p1.ap()), 16)
                    w("sync", "dX")
                run_phase(nc, allsems["TL"], TAILC2)
                return nc

            des = ExitStack()
            with des:
                SD = lambda n, sh, dt: des.enter_context(nc.sbuf_tensor(n, sh, dt))
                W1_sb = SD("W1_sb", [128, 8, 4096], BF16)
                W2_sb = SD("W2_sb", [128, 32, D], BF16)
                g2col = SD("g2col", [128, 8], F32)
                b2lb = SD("b2lb", [128, 8], BF16)

                d1es = ExitStack()
                with d1es:
                    SD1 = lambda n, sh, dt: d1es.enter_context(nc.sbuf_tensor(n, sh, dt))
                    Wo_sb = SD1("Wo_sb", [128, 8, D], BF16)
                    borowb = SD1("borowb", [1, D], BF16)
                    borow = SD1("borow", [1, D], F32)
                    pqT = [SD1("pqT0", [64, 16, 128], BF16),
                           SD1("pqT1", [64, 16, 128], BF16)]
                    xn1t3 = SD1("xn1t3", [128, D], F32)
                    p1t3 = SD1("p1t3", [128, D], F32)
                    att = SD1("att", [128, D], BF16)
                    attT = SD1("attT", [128, 8, 128], BF16)
                    xrt = SD1("xrt", [128, D], F32)
                    n2b = SD1("n2b", [128, D], BF16)
                    n2st = SD1("n2st", [128, D], BF16)
                    s2t = SD1("s2t", [128, D], F32)
                    dent = SD1("dent", [128, 16], F32)
                    rden3 = SD1("rden3", [128, 16], F32)
                    stats3 = SD1("stats3", [128, 12], F32)
                    mv3 = SD1("mv3", [128, 2], F32)
                    rstd3 = SD1("rstd3", [128, 1], F32)
                    nmr3 = SD1("nmr3", [128, 1], F32)
                    pN = [d1es.enter_context(nc.psum_tensor("d1n%d" % b_, [128, 512], F32))
                          for b_ in range(5)]
                    pTa3 = d1es.enter_context(nc.psum_tensor("d1ta", [128, 1024], BF16))
                    pTb3 = d1es.enter_context(nc.psum_tensor("d1tb", [128, 1024], BF16))
                    HB = [(h // 7, (h % 7) * 66) if h < 14 else (2, (h - 14) * 66)
                          for h in range(16)]

                    def D1(env, cur, eng):
                        SY, GP, V, SC, T = proxies(cur, eng)
                        w = mk_waiter(env, cur, eng)
                        xn1v = scr_xn1.ap().rearrange("(t p) c -> t p c", p=128)
                        p1v = scr_p1.ap().rearrange("(t p) c -> t p c", p=128)
                        s2v = scr_s2.ap().rearrange("(t p) c -> t p c", p=128)

                        wov = wo_d.ap().rearrange("(c p) o -> c p o", p=128)
                        for cc in range(8):
                            env.inc("dG", GP.dma_start(Wo_sb[:, cc, :], wov[cc]), 16)
                        w1v = w1_d.ap().rearrange("(c p) o -> c p o", p=128)
                        for cc in range(8):
                            env.inc("dG2", GP.dma_start(W1_sb[:, cc, :], w1v[cc]), 16)
                        w2v = w2_d.ap().rearrange("(c p) o -> c p o", p=128)
                        for kk in range(32):
                            env.inc("dG2", GP.dma_start(W2_sb[:, kk, :], w2v[kk]), 16)
                        env.inc("dS2", SC.dma_start(borow[:], bo_d.ap()), 16)
                        env.inc("dS2", SC.dma_start(g2col[:], g2_c_d.ap()), 16)
                        env.inc("dS2", SC.dma_start(dent[:, 0:8], b2l_c_d.ap()), 16)
                        w("vector", "dS2")
                        V.tensor_copy(borowb[:], borow[:])
                        env.inc("eV", V.tensor_copy(b2lb[:], dent[:, 0:8]), 1)
                        V.drain()

                        NTL = S // 128
                        ev = {"num": {}, "att": {}, "xr": {}, "s2": {}, "n2T": {},
                              "pq_ld": {}, "xn_ld": {}, "p1_ld": {}, "n2st": {}}
                        env.inc("dX", SY.dma_start(pqT[0][0:64, 0:8, :],
                                                   scr_phiqT.ap()[:, 0:8, 0:128]), 16)
                        env.inc("dX", SY.dma_start(pqT[0][0:64, 8:16, :],
                                                   scr_phiqT.ap()[:, 8:16, 0:128]), 16)
                        ev["pq_ld"][0] = ("dX", env.val("dX"))

                        def emit_n2T(t):
                            w("tensor", "eS", ev["n2act"][t])
                            ptb2 = pTb3 if t % 2 == 0 else pTa3
                            ins = None
                            for cb in range(8):
                                ins = T.transpose(ptb2[:, cb * 128:(cb + 1) * 128],
                                                  n2b[:, cb * 128:(cb + 1) * 128], ident[:])
                            ev["n2T"][t] = env.inc("eT", ins, 1)
                            w("scalar", "eT")
                            w("scalar", "dS")
                            ev["n2st"][t] = env.inc("eS", SC.copy(n2st[:], ptb2[:]), 1)
                            SC.drain()
                            for cb in range(8):
                                env.inc("dS", SC.dma_start(
                                    scr_y0T.ap()[cb, :, t * 128:(t + 1) * 128],
                                    n2st[:, cb * 128:(cb + 1) * 128]), 16)

                        for i in range(NTL):
                            par = i % 2
                            w("sync", "eV", ev["xr"].get(i - 1, 0))
                            env.inc("dW", SY.dma_start(xn1t3[:], xn1v[i]), 16)
                            ev["xn_ld"][i] = env.val("dW")
                            w("sync", "eV", ev["s2"].get(i - 1, 0))
                            env.inc("dW2", SY.dma_start(p1t3[:], p1v[i]), 16)
                            ev["p1_ld"][i] = env.val("dW2")
                            if i + 1 < NTL:
                                sm = "dX" if (i + 1) % 2 == 0 else "dX2"
                                w("sync", "eT", ev["num"].get(i - 1, 0))
                                env.inc(sm, SY.dma_start(
                                    pqT[1 - par][0:64, 0:8, :],
                                    scr_phiqT.ap()[:, 0:8, (i + 1) * 128:(i + 2) * 128]), 16)
                                env.inc(sm, SY.dma_start(
                                    pqT[1 - par][0:64, 8:16, :],
                                    scr_phiqT.ap()[:, 8:16, (i + 1) * 128:(i + 2) * 128]), 16)
                                ev["pq_ld"][i + 1] = (sm, env.val(sm))

                            w("tensor", ev["pq_ld"][i][0], ev["pq_ld"][i][1])
                            w("tensor", "eV", ev["att"].get(i - 1, 0))
                            ins = None
                            for h in range(16):
                                b_, off = HB[h]
                                ins = T.matmul(pN[b_][:, off:off + 65],
                                               pqT[par][0:64, h, :],
                                               kv16b[0:64, h * 66:h * 66 + 65],
                                               start=True, stop=True)
                            ev["num"][i] = env.inc("eT", ins, 1)
                            w("vector", "eT", ev["num"][i])
                            for (b_, h0, nh) in ((0, 0, 7), (1, 7, 7), (2, 14, 2)):
                                ins = V.tensor_copy(
                                    dent[:, h0:h0 + nh],
                                    bass.AP(pN[b_], 64, [[512, 128], [66, nh], [1, 1]]))
                            env.inc("eV", ins, 1)
                            V.drain()
                            env.inc("eV", V.reciprocal(rden3[:], dent[:]), 1)
                            V.drain()
                            for (b_, h0, nh) in ((0, 0, 7), (1, 7, 7), (2, 14, 2)):
                                ins = V.tensor_tensor(
                                    att[:, h0 * 64:(h0 + nh) * 64].rearrange(
                                        "p (h x) -> p h x", x=64),
                                    bass.AP(pN[b_], 0, [[512, 128], [66, nh], [1, 64]]),
                                    rden3[:, h0:h0 + nh].unsqueeze(-1).broadcast_to((128, nh, 64)),
                                    ALU.mult)
                            ev["att"][i] = env.inc("eV", ins, 1)

                            w("tensor", "eV", ev["att"][i])
                            w("tensor", "eS", ev["n2st"].get(i - 3, 0))
                            ptb = pTa3 if par == 0 else pTb3
                            ins = None
                            for cb in range(8):
                                ins = T.transpose(ptb[:, cb * 128:(cb + 1) * 128],
                                                  att[:, cb * 128:(cb + 1) * 128], ident[:])
                            env.inc("eT", ins, 1)
                            w("vector", "eT")
                            env.inc("eV", V.tensor_copy(
                                attT[:], ptb[:].rearrange("p (c s) -> p c s", s=128)), 1)
                            ev_attT = env.val("eV")

                            w("tensor", "eV", ev_attT)
                            w("tensor", "dG", 16 * 8)
                            for half in range(2):
                                for cc in range(8):
                                    T.matmul(pN[3 + half][:], attT[:, cc, :],
                                             Wo_sb[:, cc, half * 512:(half + 1) * 512],
                                             start=(cc == 0), stop=False)
                                ins = T.matmul(pN[3 + half][:], onesb[:],
                                               borowb[0:1, half * 512:(half + 1) * 512],
                                               start=False, stop=True)
                            ev_wo = env.inc("eT", ins, 1)
                            if i >= 1:
                                emit_n2T(i - 1)
                            w("vector", "eT", ev_wo)
                            w("vector", "dW", ev["xn_ld"][i])
                            w("vector", "eS", ev.get("n2act", {}).get(i - 1, 0))
                            V.tensor_tensor(xrt[:, 0:512], xn1t3[:, 0:512], pN[3][:], ALU.add)
                            ev["xr"][i] = env.inc("eV", V.tensor_tensor(
                                xrt[:, 512:1024], xn1t3[:, 512:1024], pN[4][:], ALU.add), 1)
                            V.drain()
                            w("scalar", "eT", ev["n2T"].get(i - 1, 0))
                            emit_ln(env, V, SC, w, xrt[:], n2b[:], stats3, mv3, rstd3, nmr3, 1e-6)
                            ev_n2 = env.val("eS")
                            ev.setdefault("n2act", {})[i] = ev_n2
                            w("vector", "dW2", ev["p1_ld"][i])
                            w("vector", "dS")
                            ev["s2"][i] = env.inc("eV", V.tensor_tensor(
                                s2t[:], p1t3[:], xrt[:], ALU.add), 1)
                            w("scalar", "eV", ev["s2"][i])
                            env.inc("dS", SC.dma_start(s2v[i], s2t[:]), 16)

                        emit_n2T(NTL - 1)
                        w("scalar", "dS")
                        w("sync", "dX")
                        w("sync", "dW")
                        w("gpsimd", "dG")
                        w("gpsimd", "dG2")

                    run_phase(nc, allsems["D1"], D1)

                d2es = ExitStack()
                with d2es:
                    SD2 = lambda n, sh, dt: d2es.enter_context(nc.sbuf_tensor(n, sh, dt))
                    y0T_t = [SD2("y0T_t0", [128, 8, 128], BF16),
                             SD2("y0T_t1", [128, 8, 128], BF16)]
                    s2t2 = SD2("s2t2", [128, D], F32)
                    y1b = SD2("y1b", [128, 4096], BF16)
                    y1T_t = SD2("y1T_t", [128, 32, 128], BF16)
                    outt = SD2("outt", [128, D], F32)
                    b2row_f = SD2("b2row_f", [1, D], F32)
                    b1row = SD2("b1row", [1, 4096], F32)
                    b1rowb = SD2("b1rowb", [1, 4096], BF16)
                    b2rowb = SD2("b2rowb", [1, D], BF16)
                    pE = [d2es.enter_context(nc.psum_tensor("d2p%d" % b_, [128, 512], F32))
                          for b_ in range(4)]
                    pTa4 = d2es.enter_context(nc.psum_tensor("d2ta", [128, 1024], BF16))
                    pTb4 = d2es.enter_context(nc.psum_tensor("d2tb", [128, 1024], BF16))

                    def D2(env, cur, eng):
                        SY, GP, V, SC, T = proxies(cur, eng)
                        w = mk_waiter(env, cur, eng)
                        y0v = scr_y0T.ap()
                        s2v = scr_s2.ap().rearrange("(t p) c -> t p c", p=128)
                        outv = out_d.ap().rearrange("(t p) c -> t p c", p=128)

                        env.inc("dS2", SC.dma_start(b1row[:], b1_d.ap()), 16)
                        env.inc("dS2", SC.dma_start(b2row_f[:], b2_d.ap()), 16)
                        ins = None
                        for fch in range(8):
                            w("tensor", "eV")
                            for cc in range(8):
                                ins = T.matmul(pE[fch % 2][0:1, 0:512],
                                               b2lb[:, cc:cc + 1],
                                               W1_sb[:, cc, fch * 512:(fch + 1) * 512],
                                               start=(cc == 0), stop=(cc == 7))
                            env.inc("eT", ins, 1)
                            w("vector", "eT")
                            w("vector", "dS2")
                            env.inc("eV", V.tensor_tensor(
                                b1row[0:1, fch * 512:(fch + 1) * 512],
                                b1row[0:1, fch * 512:(fch + 1) * 512],
                                pE[fch % 2][0:1, 0:512], ALU.add), 1)
                        V.drain()
                        V.tensor_copy(b1rowb[:], b1row[:])
                        env.inc("eV", V.tensor_copy(b2rowb[:], b2row_f[:]), 1)
                        V.drain()
                        for cc in range(8):
                            ins = V.tensor_scalar(W1_sb[:, cc, :], W1_sb[:, cc, :],
                                                  g2col[:, cc:cc + 1], None, ALU.mult)
                        env.inc("eV", ins, 1)
                        V.drain()
                        ev_w1 = env.val("eV")

                        NTL = S // 128
                        ev = {"w1": {}, "gelu": {}, "y1T": {}, "w2": {}, "out": {},
                              "y0_ld": {}, "s2_ld": {}}
                        for cb in range(8):
                            env.inc("dX", SY.dma_start(y0T_t[0][:, cb, :],
                                                       y0v[cb, :, 0:128]), 16)
                        ev["y0_ld"][0] = env.val("dX")

                        for i in range(NTL):
                            par = i % 2
                            w("sync", "eV", ev["out"].get(i - 1, 0))
                            env.inc("dW", SY.dma_start(s2t2[:], s2v[i]), 16)
                            ev["s2_ld"][i] = env.val("dW")

                            for fch in range(8):
                                pb_ = pE[fch % 2]
                                w("tensor", "dX", ev["y0_ld"][i])
                                w("tensor", "eV", ev_w1)
                                w("tensor", "eS", ev["gelu"].get((i, fch - 2),
                                                                 ev["gelu"].get((i - 1, 6 + (fch % 2)), 0)))
                                for cc in range(8):
                                    T.matmul(pb_[:], y0T_t[par][:, cc, :],
                                             W1_sb[:, cc, fch * 512:(fch + 1) * 512],
                                             start=(cc == 0), stop=False)
                                ins = T.matmul(pb_[:], onesb[:],
                                               b1rowb[0:1, fch * 512:(fch + 1) * 512],
                                               start=False, stop=True)
                                ev["w1"][(i, fch)] = env.inc("eT", ins, 1)
                                w("scalar", "eT", ev["w1"][(i, fch)])
                                ev["gelu"][(i, fch)] = env.inc("eS", SC.activation(
                                    y1b[:, fch * 512:(fch + 1) * 512],
                                    pb_[:], AF.Gelu_apprx_tanh), 1)
                            if i + 1 < NTL:
                                w("sync", "eT", ev["w1"][(i, 7)])
                                for cb in range(8):
                                    env.inc("dX", SY.dma_start(
                                        y0T_t[1 - par][:, cb, :],
                                        y0v[cb, :, (i + 1) * 128:(i + 2) * 128]), 16)
                                ev["y0_ld"][i + 1] = env.val("dX")

                            for grp in range(4):
                                ptb = pTa4 if grp % 2 == 0 else pTb4
                                w("tensor", "eS", ev["gelu"][(i, 2 * grp + 1)])
                                w("tensor", "eV", ev["y1T"].get((i, grp - 2),
                                                                ev["y1T"].get((i - 1, grp + 2 - 4 * (grp >= 2)), 0)))
                                ins = None
                                for b_ in range(8):
                                    blk = grp * 8 + b_
                                    ins = T.transpose(ptb[:, b_ * 128:(b_ + 1) * 128],
                                                      y1b[:, blk * 128:(blk + 1) * 128], ident[:])
                                env.inc("eT", ins, 1)
                                w("vector", "eT")
                                ev["y1T"][(i, grp)] = env.inc("eV", V.tensor_copy(
                                    y1T_t[:, grp * 8:(grp + 1) * 8, :],
                                    ptb[:].rearrange("p (c s) -> p c s", s=128)), 1)

                            w("tensor", "eV", ev["y1T"][(i, 3)])
                            for half in range(2):
                                for kk in range(32):
                                    T.matmul(pE[2 + half][:], y1T_t[:, kk, :],
                                             W2_sb[:, kk, half * 512:(half + 1) * 512],
                                             start=(kk == 0), stop=False)
                                ins = T.matmul(pE[2 + half][:], onesb[:],
                                               b2rowb[0:1, half * 512:(half + 1) * 512],
                                               start=False, stop=True)
                            ev["w2"][i] = env.inc("eT", ins, 1)
                            w("vector", "eT", ev["w2"][i])
                            w("vector", "dW", ev["s2_ld"][i])
                            w("vector", "dS")
                            V.tensor_tensor(outt[:, 0:512], s2t2[:, 0:512], pE[2][:], ALU.add)
                            ev["out"][i] = env.inc("eV", V.tensor_tensor(
                                outt[:, 512:1024], s2t2[:, 512:1024], pE[3][:], ALU.add), 1)
                            w("scalar", "eV", ev["out"][i])
                            env.inc("dS", SC.dma_start(outv[i], outt[:]), 16)

                        w("scalar", "dS")
                        w("sync", "dX")
                        w("sync", "dW")

                    run_phase(nc, allsems["D2"], D2)

        return nc


def _host_inputs(inputs, S=4096):
    f32 = np.float32
    bf = ml_dtypes.bfloat16
    x = np.asarray(inputs["L0_attn_x"], dtype=f32)

    def col8(a):
        return np.ascontiguousarray(np.asarray(a, dtype=f32).reshape(8, 128).T)

    base = {
        "w_l0": np.ascontiguousarray(np.asarray(inputs["L0_attn_W"], dtype=f32)),
        "k8": np.ascontiguousarray(np.asarray(inputs["L0_ffn_kernel"]).astype(bf)),
        "scales8": np.array([[np.asarray(inputs["L0_ffn_input_scale"], dtype=f32).reshape(()),
                              np.asarray(inputs["L0_ffn_kernel_scale"], dtype=f32).reshape(())]], dtype=f32),
        "bias8": np.asarray(inputs["L0_ffn_bias"]).astype(bf).reshape(1, D),
        "wq": np.ascontiguousarray(np.asarray(inputs["L1_attn_Wq"], dtype=f32).reshape(D, D)),
        "wk": np.ascontiguousarray(np.asarray(inputs["L1_attn_Wk"], dtype=f32).reshape(D, D)),
        "wv": np.ascontiguousarray(np.asarray(inputs["L1_attn_Wv"], dtype=f32).reshape(D, D)),
        "bk": np.asarray(inputs["L1_attn_bk"], dtype=f32).reshape(1, D),
        "bv": np.asarray(inputs["L1_attn_bv"], dtype=f32).reshape(1, D),
        "wo": np.ascontiguousarray(np.asarray(inputs["L1_attn_Wo"], dtype=f32).reshape(D, D)),
        "bo": np.asarray(inputs["L1_attn_bo"], dtype=f32).reshape(1, D),
        "w1": np.ascontiguousarray(np.asarray(inputs["L1_attn_W1"], dtype=f32)),
        "b1": np.asarray(inputs["L1_attn_b1"], dtype=f32).reshape(1, 4096),
        "w2": np.ascontiguousarray(np.asarray(inputs["L1_attn_W2"], dtype=f32)),
        "b2": np.asarray(inputs["L1_attn_b2"], dtype=f32).reshape(1, D),
        "wg": np.ascontiguousarray(np.asarray(inputs["L1_ffn_w_gate"], dtype=f32)),
        "wu": np.ascontiguousarray(np.asarray(inputs["L1_ffn_w_up"], dtype=f32)),
        "wd": np.ascontiguousarray(np.asarray(inputs["L1_ffn_w_down"], dtype=f32)),
        "ident": np.eye(128, dtype=bf),
        "mask": np.ascontiguousarray(np.triu(np.full((128, 128), NEG, dtype=f32), 1)),
        "bq_col": col8(inputs["L1_attn_bq"]),
        "g1_col": col8(inputs["L1_attn_ln1_scale"]),
        "b1l_col": col8(inputs["L1_attn_ln1_bias"]),
        "g2_col": col8(inputs["L1_attn_ln2_scale"]),
        "b2l_col": col8(inputs["L1_attn_ln2_bias"]),
    }
    in_maps = []
    for b in range(x.shape[0]):
        m = dict(base)
        m["x0"] = np.ascontiguousarray(x[b, :S])
        in_maps.append(m)
    return in_maps


def kernel(**inputs):
    nc = build_program(S=4096)
    in_maps = _host_inputs(inputs, S=4096)
    res = run_bass_kernel_spmd(nc, in_maps, core_ids=list(range(8)))
    return np.stack([np.asarray(r["out"], dtype=np.float32) for r in res.results])
